# version 14
# baseline (speedup 1.0000x reference)
import numpy as np
import concourse.bass as bass
import concourse.mybir as mybir
from concourse.bass_utils import run_bass_kernel_spmd

F32 = mybir.dt.float32
BF16 = mybir.dt.bfloat16
AF = mybir.ActivationFunctionType
ALU = mybir.AluOpType
AX = mybir.AxisListType

NCORES = 8
T = 2048
D = 2048
NH = 16
DH = 128
BLK = 256
NB = T // BLK
G = 2
NG = NH // G
E = 8
H = 1408
HC = H // 128
HS = 2816
HSC = HS // 128
KC = D // 128
LN_EPS = 1e-5
RMS_EPS = 1e-6

DBG = False



class Plan:
    ENGINES = ("sync", "tensor", "vector", "scalar", "gpsimd")

    def __init__(self):
        self.items = {e: [] for e in self.ENGINES}
        self.ctr = {}

    def on(self, eng, fn, waits=(), inc=None, inc_k=1):
        waits = [w for w in waits if w is not None]
        for s, v in waits:
            have = self.ctr.get(s, 0)
            assert v <= have, f"wait on {s}>={v} but only {have} planned"
        if inc is not None:
            self.ctr[inc] = self.ctr.get(inc, 0) + inc_k
        self.items[eng].append((waits, fn, inc, inc_k))
        return (inc, self.ctr[inc]) if inc is not None else None


class Bank:
    def __init__(self, ap):
        self.ap = ap
        self.wr = None
        self.rd = None



def _prep_shared(wqkv, wgate, wout, gate_w, ffn_wg, ffn_wu, ffn_wd, swg, swu, swd, norm_w):
    f32 = np.float32

    def lhsT_blocks(WT, nblk):
        Din, O = WT.shape
        assert O == nblk * 128
        return np.ascontiguousarray(WT.reshape(Din, nblk, 128).transpose(1, 0, 2))

    wqkvT = np.ascontiguousarray(np.asarray(wqkv, f32).T)
    Wq = lhsT_blocks(wqkvT[:, 0:2048], 16)
    Wk = lhsT_blocks(wqkvT[:, 2048:4096], 16)
    Wv = np.ascontiguousarray(
        wqkvT[:, 4096:6144].reshape(D, NG, G * 128).transpose(1, 0, 2))
    WgateT = lhsT_blocks(np.ascontiguousarray(np.asarray(wgate, f32).T), 16)
    WoutT = np.ascontiguousarray(np.asarray(wout, f32).T)
    Wg = np.ascontiguousarray(np.asarray(ffn_wg, f32).reshape(E, D, HC, 128).transpose(0, 2, 1, 3))
    Wu = np.ascontiguousarray(np.asarray(ffn_wu, f32).reshape(E, D, HC, 128).transpose(0, 2, 1, 3))
    Wd = np.ascontiguousarray(np.asarray(ffn_wd, f32))
    Swg = lhsT_blocks(np.ascontiguousarray(np.asarray(swg, f32)), HSC)
    Swu = lhsT_blocks(np.ascontiguousarray(np.asarray(swu, f32)), HSC)
    Swd = np.ascontiguousarray(np.asarray(swd, f32))
    gw_t = np.ascontiguousarray(
        np.asarray(gate_w, f32).reshape(16, 128, 8).transpose(1, 0, 2).reshape(128, 128))
    normw_t = np.ascontiguousarray(np.asarray(norm_w, f32).reshape(16, 128).T)

    ident = np.eye(128, dtype=f32)
    ident16 = np.eye(16, dtype=f32)
    ones16x128 = np.ones((16, 128), f32)
    ar16 = np.broadcast_to(np.arange(1, BLK + 1, dtype=f32)[None, :], (16, BLK)).copy()
    i_idx = np.arange(BLK)
    m0 = (i_idx[None, :] >= np.arange(0, 128)[:, None]).astype(f32)
    m1 = (i_idx[None, :] >= np.arange(128, 256)[:, None]).astype(f32)
    mask = np.ascontiguousarray(np.concatenate([m0, m1], axis=1))
    sel16 = np.zeros((16, 16 * 128), f32)
    for h in range(16):
        sel16[h, h * 128:(h + 1) * 128] = 1.0
    sel8 = np.zeros((8, 8 * 128), f32)
    for e in range(8):
        sel8[e, e * 128:(e + 1) * 128] = 1.0
    return dict(Wq=Wq, Wk=Wk, Wv=Wv, WgateT=WgateT, WoutT=WoutT, Wg=Wg, Wu=Wu, Wd=Wd,
                Swg=Swg, Swu=Swu, Swd=Swd, gw_t=gw_t, normw_t=normw_t, ident=ident,
                ident16=ident16, ones16x128=ones16x128, ar16=ar16, mask=mask,
                sel16=sel16, sel8=sel8)


def _prep_core(core, x_full, slope):
    f32 = np.float32
    xs = np.ascontiguousarray(np.asarray(x_full, f32).reshape(NCORES, T, D)[core])
    prevdec = np.full((16, 7), 1e7, f32)
    for cp in range(core):
        prevdec[:, cp] = 2048.0 * (core - 1 - cp)
    return dict(x=xs, prevdec=prevdec,
                slope=np.ascontiguousarray(np.asarray(slope, f32).reshape(16, 1)))



def build_program(dbg=False, phases=("p1", "comb", "attn", "ffn")):
    nc = bass.Bass(target_bir_lowering=False, debug=False, detect_race_conditions=False)

    dx = nc.dram_tensor("x", [T, D], F32, kind="ExternalInput").ap()
    dslope = nc.dram_tensor("slope", [16, 1], F32, kind="ExternalInput").ap()
    dprevdec = nc.dram_tensor("prevdec", [16, 7], F32, kind="ExternalInput").ap()
    dWq = nc.dram_tensor("Wq", [16, D, 128], F32, kind="ExternalInput").ap()
    dWk = nc.dram_tensor("Wk", [16, D, 128], F32, kind="ExternalInput").ap()
    dWv = nc.dram_tensor("Wv", [NG, D, G * 128], F32, kind="ExternalInput").ap()
    dWgateT = nc.dram_tensor("WgateT", [16, D, 128], F32, kind="ExternalInput").ap()
    dWoutT = nc.dram_tensor("WoutT", [D, D], F32, kind="ExternalInput").ap()
    dWg = nc.dram_tensor("Wg", [E, HC, D, 128], F32, kind="ExternalInput").ap()
    dWu = nc.dram_tensor("Wu", [E, HC, D, 128], F32, kind="ExternalInput").ap()
    dWd = nc.dram_tensor("Wd", [E, H, D], F32, kind="ExternalInput").ap()
    dSwg = nc.dram_tensor("Swg", [HSC, D, 128], F32, kind="ExternalInput").ap()
    dSwu = nc.dram_tensor("Swu", [HSC, D, 128], F32, kind="ExternalInput").ap()
    dSwd = nc.dram_tensor("Swd", [HS, D], F32, kind="ExternalInput").ap()
    dgw = nc.dram_tensor("gw_t", [128, 128], F32, kind="ExternalInput").ap()
    dnormw = nc.dram_tensor("normw_t", [128, 16], F32, kind="ExternalInput").ap()
    dident = nc.dram_tensor("ident", [128, 128], F32, kind="ExternalInput").ap()
    dident16 = nc.dram_tensor("ident16", [16, 16], F32, kind="ExternalInput").ap()
    dones16x128 = nc.dram_tensor("ones16x128", [16, 128], F32, kind="ExternalInput").ap()
    dar16 = nc.dram_tensor("ar16", [16, BLK], F32, kind="ExternalInput").ap()
    dmask = nc.dram_tensor("mask", [128, 512], F32, kind="ExternalInput").ap()
    dsel16 = nc.dram_tensor("sel16", [16, 2048], F32, kind="ExternalInput").ap()
    dsel8 = nc.dram_tensor("sel8", [8, 1024], F32, kind="ExternalInput").ap()
    dout = nc.dram_tensor("out", [T, D], F32, kind="ExternalOutput").ap()

    qt_hbm = nc.dram_tensor("qt_hbm", [NG, NB // 2, 128, G * 512], BF16).ap()
    kvB = nc.dram_tensor("kvB", [128, NH * 128], F32)
    kvBg = nc.dram_tensor("kvBg", [NCORES * 128, NH * 128], F32)
    acc_a = nc.dram_tensor("acc_a", [T, D], F32).ap()
    acc_b = nc.dram_tensor("acc_b", [T, D], F32).ap()

    dbg_outs = {}
    if dbg:
        for name, shape in [("xn_fm", [128, KC * T]), ("comb_dbg", [128, 128]),
                            ("attn_fm", [128, NH * T]), ("kv_dbg", [128, NH * 128]),
                            ("kvin_dbg", [128, NH * 128]), ("z_fm", [128, NH * T]),
                            ("rstd_dbg", [128, T]), ("small2_dbg", [128, 32]),
                            ("acc1_dbg", [T, D]), ("acc2_dbg", [T, D]),
                            ("gc_dbg", [128, HC * T]), ("gsh_dbg", [128, HC * T])]:
            dbg_outs[name] = nc.dram_tensor(name, shape, F32, kind="ExternalOutput").ap()

    SBUF_BASE = 20480
    off = [SBUF_BASE]

    def sb(name, shape, dtype, align=32):
        o = (off[0] + align - 1) // align * align
        t = nc.alloc_sbuf_tensor_at(name, shape, dtype, offset=o)
        itemsize = 2 if dtype == BF16 else 4
        off[0] = o + int(np.prod(shape[1:])) * itemsize
        return t.ap()

    xn = sb("xn", [128, KC * T], BF16)
    kv = sb("kv", [128, NH * 128], F32)
    combT = sb("combT", [8, T], BF16)
    comb = sb("comb", [128, 16 * 8], F32)
    normw = sb("normw", [128, 16], F32)
    ident_b = sb("ident_b", [128, 128], BF16)
    ident_f = sb("ident_f", [128, 128], F32)
    ident16 = sb("ident16_s", [16, 16], F32)
    ones16x128 = sb("ones16x128_s", [16, 128], F32)
    ones_b = sb("ones_b", [128, 128], BF16)
    mask_b = sb("mask_b", [128, 512], BF16)
    sel16_b = sb("sel16_b", [16, 2048], BF16)
    sel8_b = sb("sel8_b", [8, 1024], BF16)
    slope_c = sb("slope_c", [16, 1], F32)
    prevdec_s = sb("prevdec_s", [16, 7], F32)
    ar16_s = sb("ar16_s", [16, BLK], F32)
    sar = sb("sar", [16, BLK], F32)
    qdec_all = sb("qdec_all", [16, BLK], BF16)
    kdec_all = sb("kdec_all", [16, BLK], BF16)
    lpows = sb("lpows", [16, 7], F32)
    diag16 = sb("diag16", [16, 16], F32)
    bd_bc = sb("bd_bc", [128, 16], F32)
    lam_b = sb("lam_b", [128, 7 * 16], F32)
    kv_bf = sb("kv_bf", [128, 4 * 128], BF16)
    qdec_bc = sb("qdec_bc", [128, G * 2 * BLK], BF16)
    kdec_bc = sb("kdec_bc", [128, G * 2 * BLK], BF16)
    gw_b = sb("gw_b", [128, 128], BF16)
    small1 = sb("small1", [128, 16], F32)
    small2 = sb("small2", [128, 32], F32)
    small3 = sb("small3", [128, 32], F32)
    STATIC_END = off[0]
    ARENA = (STATIC_END + 127) // 128 * 128
    ARENA_END = 228352
    assert ARENA_END - ARENA >= 104448, (STATIC_END, ARENA)

    def arena(name, shape, dtype, at):
        o = ARENA + at
        itemsize = 2 if dtype == BF16 else 4
        end = o + int(np.prod(shape[1:])) * itemsize
        assert end <= ARENA_END, (name, end)
        assert o % 32 == 0, (name, o)
        return nc.alloc_sbuf_tensor_at(name, shape, dtype, offset=o).ap()

    pb = [nc.alloc_psum_tensor(f"pb{i}", [128, 512], F32).ap() for i in range(7)]
    pt_b = nc.alloc_psum_tensor("pt_b", [128, 1024], BF16).ap()
    banks = [Bank(p) for p in pb] + [Bank(pt_b)]

    plan = Plan()
    GQ_POOL = [f"wq{i}" for i in range(6)]
    XQ_POOL = [f"xq{i}" for i in range(4)]
    SEMS = GQ_POOL + XQ_POOL + ["pe", "sc", "ve", "gp", "cc"]
    _qi = {"g": 0, "x": 0}

    def S(eng_name, sem, k=1):
        def _on(fn, waits=(), inc=sem, inc_k=k):
            return plan.on(eng_name, fn, waits, inc, inc_k)
        return _on

    TE = S("tensor", "pe")
    VE = S("vector", "ve")
    SC = S("scalar", "sc")

    def GQ(fn, waits=()):
        s = GQ_POOL[_qi["g"] % len(GQ_POOL)]
        _qi["g"] += 1
        return plan.on("gpsimd", fn, waits, inc=s, inc_k=16)

    def XQ(fn, waits=()):
        s = XQ_POOL[_qi["x"] % len(XQ_POOL)]
        _qi["x"] += 1
        return plan.on("sync", fn, waits, inc=s, inc_k=16)

    def last(sem):
        return (sem, plan.ctr.get(sem, 0))

    def all_dma_done():
        return [(s, plan.ctr[s]) for s in GQ_POOL + XQ_POOL if plan.ctr.get(s, 0)]

    def mm_group(bank, mms, extra_waits=()):
        waits = list(extra_waits)
        if bank.rd is not None:
            waits.append(bank.rd)
        n = len(mms)
        for i, (lh, rh, o) in enumerate(mms):
            o = o if o is not None else bank.ap

            def fn(t, lh=lh, rh=rh, o=o, first=(i == 0), lastm=(i == n - 1)):
                return t.matmul(o, lh, rh, start=first, stop=lastm)
            if i == n - 1:
                bank.wr = TE(fn, waits=waits)
            else:
                TE(fn, waits=waits, inc=None)
            waits = []
        return bank.wr

    def tiled(dst2d, src2d, k):
        return (dst2d.rearrange("p (k m) -> p k m", k=k),
                src2d.rearrange("(k p) m -> p k m", p=128))

    def bank_read(eng_on, bank, fn, extra_waits=()):
        w = list(extra_waits)
        if bank.wr is not None:
            w.append(bank.wr)
        m = eng_on(fn, waits=w)
        bank.rd = m
        return m

    ld0 = []
    for dst, src in [(ident_f, dident), (ident16, dident16), (ones16x128, dones16x128),
                     (ar16_s, dar16), (slope_c, dslope),
                     (prevdec_s, dprevdec), (normw, dnormw)]:
        ld0.append(XQ(lambda sy, dst=dst, src=src: sy.dma_start(out=dst, in_=src)))
    ld0g = []
    for dst, src in [(ident_b, dident), (mask_b, dmask), (sel16_b, dsel16), (gw_b, dgw),
                     (sel8_b, dsel8)]:
        ld0g.append(GQ(lambda g, dst=dst, src=src: g.dma_start(out=dst, in_=src)))

    TE(lambda t: t.nop(), waits=ld0 + ld0g, inc=None)

    VE(lambda v: v.memset(ones_b, 1.0))
    VE(lambda v: v.memset(small1[:, 1:2], LN_EPS))
    VE(lambda v: v.memset(small1[:, 2:3], RMS_EPS))
    m_eps = last("ve")
    VE(lambda v: v.tensor_scalar(sar, ar16_s, slope_c, None, ALU.mult), waits=ld0)
    m_sar = last("ve")
    SC(lambda s: s.activation(qdec_all, sar, AF.Exp, scale=-1.0), waits=[m_sar])
    SC(lambda s: s.activation(kdec_all, sar, AF.Exp, scale=1.0))
    VE(lambda v: v.tensor_scalar(lpows, prevdec_s, slope_c, None, ALU.mult), waits=ld0)
    SC(lambda s: s.activation(lpows, lpows, AF.Exp, scale=-1.0), waits=[last("ve")])
    SC(lambda s: s.activation(small1[0:16, 0:1], slope_c, AF.Exp, scale=-256.0))
    m_exp = last("sc")

    VE(lambda v: v.tensor_scalar(diag16, ident16, small1[0:16, 0:1], None, ALU.mult),
       waits=[m_exp] + ld0)
    m_diag = last("ve")
    mm_group(banks[5], [(ones16x128, diag16, pb[5][:, 0:16])], extra_waits=[m_diag])
    bank_read(VE, banks[5], lambda v: v.tensor_copy(bd_bc, pb[5][:, 0:16]))
    for cp in range(7):
        VE(lambda v, cp=cp: v.tensor_scalar(diag16, ident16, lpows[:, cp:cp + 1], None, ALU.mult),
           waits=[m_exp])
        m_d = last("ve")
        mm_group(banks[5], [(ones16x128, diag16, pb[5][:, 0:16])], extra_waits=[m_d])
        bank_read(VE, banks[5],
                  lambda v, cp=cp: v.tensor_copy(lam_b[:, cp * 16:(cp + 1) * 16], pb[5][:, 0:16]))
    VE(lambda v: v.memset(kv, 0.0))

    lnst = [small2[:, 0:8], small2[:, 8:16]]
    xt = [arena("xt0", [128, D], F32, 32768), arena("xt1", [128, D], F32, 40960)]
    xc = arena("xc", [128, D], F32, 49152)
    xn_tm = [arena(f"xn_tm{i}", [128, D], BF16, 57344 + i * 4096) for i in range(2)]

    xt_rd = [None, None]
    m_xc_prev = None
    m_lnst_prev = [None, None]
    m_tr_prev = [None, None]
    m_xn_tc = []
    for tc in range(16):
        buf = tc % 2
        ln = lnst[buf]
        m_ld = XQ(lambda sy, tc=tc, buf=buf: sy.dma_start(out=xt[buf],
                                                          in_=dx[tc * 128:(tc + 1) * 128, :]),
                  waits=[xt_rd[buf]])
        VE(lambda v, buf=buf, ln=ln: v.reduce_sum(ln[:, 0:1], xt[buf], axis=AX.X),
           waits=[m_ld, m_lnst_prev[buf]])
        m_sum = last("ve")
        VE(lambda v, ln=ln: v.tensor_scalar(ln[:, 1:2], ln[:, 0:1], -1.0 / D, None, ALU.mult),
           waits=[m_sum])
        m_nm = last("ve")
        VE(lambda v, buf=buf, ln=ln: v.tensor_scalar(xc, xt[buf], ln[:, 1:2], None, ALU.add),
           waits=[m_xc_prev, m_nm])
        m_xc_w = last("ve")
        SC(lambda s, buf=buf, ln=ln: s.activation(xt[buf], xc, AF.Square, accum_out=ln[:, 2:3]),
           waits=[m_xc_w])
        xt_rd[buf] = last("sc")
        SC(lambda s, ln=ln: s.activation(ln[:, 3:4], ln[:, 2:3], AF.Ln, scale=1.0 / D,
                                         bias=small1[:, 1:2]), waits=[m_eps, xt_rd[buf]])
        m_ln = last("sc")
        SC(lambda s, ln=ln: s.activation(ln[:, 4:5], ln[:, 3:4], AF.Exp, scale=-0.5),
           waits=[m_ln])
        m_rstd = last("sc")
        m_lnst_prev[buf] = m_rstd
        VE(lambda v, buf=buf, ln=ln: v.tensor_scalar(xn_tm[buf], xc, ln[:, 4:5], None, ALU.mult),
           waits=[m_rstd, m_tr_prev[buf]])
        m_xn = last("ve")
        m_xc_prev = m_xn
        tb = banks[7]
        for half in range(2):
            mws = [m_xn]
            if tb.rd is not None:
                mws.append(tb.rd)
            for j in range(8):
                k = half * 8 + j

                def fn(t, k=k, j=j, buf=buf):
                    return t.transpose(pt_b[:, j * 128:(j + 1) * 128],
                                       xn_tm[buf][:, k * 128:(k + 1) * 128], ident_b)
                if j == 7:
                    tb.wr = TE(fn, waits=mws)
                else:
                    TE(fn, waits=mws, inc=None)
                mws = []

            def cp(s, tc=tc, half=half):
                dst = xn.rearrange("p (k t) -> p k t", k=KC)[:, half * 8:(half + 1) * 8,
                                                             tc * 128:(tc + 1) * 128]
                src = pt_b.rearrange("p (j t) -> p j t", j=8)
                return s.copy(dst, src)
            bank_read(SC, tb, cp)
        m_tr_prev[buf] = tb.wr
        m_xn_tc.append(last("sc"))
    m_xn_done = last("sc")

    comb_state = {"m_comb": None, "m_combT": None}

    def emit_comb():
        scores3 = small3[:, 0:8]
        msk3 = small3[:, 8:16]
        c2t3 = small3[:, 16:24]
        stat3 = small3[:, 24:28]
        m_comb = None
        xnv = xn.rearrange("p (k t) -> p k t", k=KC)
        for tc in range(16):
            bk = banks[6]
            mms = [(xnv[:, k, tc * 128:(tc + 1) * 128], gw_b[:, k * 8:(k + 1) * 8],
                    pb[6][:, 0:8]) for k in range(KC)]
            mm_group(bk, mms, extra_waits=[m_xn_done])
            bank_read(VE, bk, lambda v: v.reduce_max(stat3[:, 0:1], pb[6][:, 0:8], axis=AX.X),
                      extra_waits=[m_comb])
            VE(lambda v: v.tensor_scalar(stat3[:, 1:2], stat3[:, 0:1], -1.0, None, ALU.mult),
               waits=[last("ve")])
            m_negmax = last("ve")
            m_exp2 = bank_read(SC, bk,
                               lambda s: s.activation(scores3, pb[6][:, 0:8], AF.Exp,
                                                      bias=stat3[:, 1:2]),
                               extra_waits=[m_negmax])
            VE(lambda v: v.reduce_sum(stat3[:, 2:3], scores3, axis=AX.X), waits=[m_exp2])
            VE(lambda v: v.reciprocal(stat3[:, 3:4], stat3[:, 2:3]), waits=[last("ve")])
            VE(lambda v: v.tensor_scalar(scores3, scores3, stat3[:, 3:4], None, ALU.mult),
               waits=[last("ve")])
            VE(lambda v: v.reduce_max(stat3[:, 0:1], scores3, axis=AX.X), waits=[last("ve")])
            VE(lambda v, tc=tc: v.tensor_scalar(comb[:, tc * 8:(tc + 1) * 8], scores3,
                                                stat3[:, 0:1], None, ALU.is_ge), waits=[last("ve")])
            VE(lambda v, tc=tc: v.scalar_tensor_tensor(msk3, comb[:, tc * 8:(tc + 1) * 8],
                                                       -1e30, scores3, ALU.mult, ALU.add),
               waits=[last("ve")])
            VE(lambda v: v.reduce_max(stat3[:, 1:2], msk3, axis=AX.X), waits=[last("ve")])
            VE(lambda v: v.tensor_scalar(c2t3, msk3, stat3[:, 1:2], None, ALU.is_ge),
               waits=[last("ve")])
            VE(lambda v, tc=tc: v.tensor_add(comb[:, tc * 8:(tc + 1) * 8],
                                             comb[:, tc * 8:(tc + 1) * 8], c2t3),
               waits=[last("ve")])
            VE(lambda v, tc=tc: v.tensor_mul(comb[:, tc * 8:(tc + 1) * 8],
                                             comb[:, tc * 8:(tc + 1) * 8], scores3),
               waits=[last("ve")])
            m_comb = last("ve")
        for q in range(4):
            bk = banks[5]
            mws = [m_comb]
            if bk.rd is not None:
                mws.append(bk.rd)
            for j in range(4):
                tc = q * 4 + j

                def fn(t, tc=tc, j=j):
                    return t.transpose(pb[5][0:8, j * 128:(j + 1) * 128],
                                       comb[:, tc * 8:(tc + 1) * 8], ident_f)
                if j == 3:
                    bk.wr = TE(fn, waits=mws)
                else:
                    TE(fn, waits=mws, inc=None)
                mws = []
            bank_read(VE, bk,
                      lambda v, q=q: v.tensor_copy(combT[:, q * 512:(q + 1) * 512], pb[5][0:8, 0:512]))
        comb_state["m_comb"] = m_comb
        comb_state["m_combT"] = last("ve")

    if "comb" in phases and "attn" not in phases:
        emit_comb()
    m_comb = comb_state["m_comb"]
    m_combT = comb_state["m_combT"]

    m_attn_sc = None
    m_kvin = None
    m_attn_final = None
    if "attn" in phases:
        attn = arena("attn", [128, NH * T], BF16, 0)
        wq_buf = [arena(f"wqb{i}", [128, KC * 128], BF16, 65536 + i * 4096) for i in range(2)]
        wk_buf = [arena(f"wkb{i}", [128, KC * 128], BF16, 73728 + i * 4096) for i in range(2)]
        wv_buf = arena("wvb", [128, KC * 256], BF16, 81920)
        qt_t = [arena(f"qtt{i}", [128, G * 512], BF16, 90112 + i * 6144) for i in range(2)]
        kt_t = [arena(f"ktt{i}", [128, G * 512], BF16, 92160 + i * 6144) for i in range(2)]
        v_t = [arena(f"vtt{i}", [128, 4 * 256], BF16, 94208 + i * 6144) for i in range(2)]
        scm_sb = arena("scm_sb", [128, 4 * 512], BF16, 102400)
        ktT_sb = arena("ktT_sb", [128, 4 * 256], BF16, 106496)
        kvtmp = arena("kvtmp", [128, 128], F32, 108544)

        xnv3 = xn.rearrange("p (k t) -> p k t", k=KC)
        m_w_rd = [None, None, None]
        m_tile_rd = [None, None]
        m_qt_dma = [None, None]
        m_kvup = {}
        m_silv_rd = None
        m_kvbf_rd = None
        first_gq_wait = []

        for g in range(NG):
            gw_waits = list(first_gq_wait) + [m_w_rd[0], m_w_rd[1], m_w_rd[2]]
            first_gq_wait = []
            m_wld = []
            for gl in range(G):
                h = g * G + gl
                dq, sq_ = tiled(wq_buf[gl], dWq[h], KC)
                m_wld.append(GQ(lambda gq, dq=dq, sq_=sq_: gq.dma_start(out=dq, in_=sq_),
                                waits=gw_waits))
                gw_waits = []
                dk, sk_ = tiled(wk_buf[gl], dWk[h], KC)
                m_wld.append(GQ(lambda gq, dk=dk, sk_=sk_: gq.dma_start(out=dk, in_=sk_)))
            dv, sv_ = tiled(wv_buf, dWv[g], KC)
            m_wld.append(GQ(lambda gq, dv=dv, sv_=sv_: gq.dma_start(out=dv, in_=sv_)))

            for gl in range(G):
                h = g * G + gl
                for tab, dst in [(qdec_all, qdec_bc), (kdec_all, kdec_bc)]:
                    bw = [m_exp]
                    if banks[5].rd is not None:
                        bw.append(banks[5].rd)
                    TE(lambda t, h=h, tab=tab: t.matmul(
                        pb[5][:, 0:256], sel16_b[:, h * 128:(h + 1) * 128], tab,
                        start=True, stop=True), waits=bw, inc=None)
                    banks[5].wr = TE(lambda t, h=h, tab=tab: t.matmul(
                        pb[5][:, 256:512], sel16_b[:, h * 128:(h + 1) * 128], tab,
                        start=True, stop=True))
                    bank_read(VE, banks[5],
                              lambda v, gl=gl, dst=dst: v.tensor_copy(
                                  dst[:, gl * 512:(gl + 1) * 512], pb[5][:, 0:512]))
            m_dec = last("ve")

            def emit_proj(bp):
                par = bp % 2
                tok = slice(bp * 512, (bp + 1) * 512)
                pw = list(m_wld) + [m_dec, m_tile_rd[par], m_qt_dma[par],
                                    m_xn_tc[bp * 4 + 3]]
                pbi = 0
                m_v = [None, None]

                def v_group(tc2):
                    nonlocal pbi, pw
                    bk = banks[pbi]
                    o = bk.ap[:, 0:256]
                    tsl = slice((bp * 4 + tc2) * 128, (bp * 4 + tc2 + 1) * 128)
                    mms = [(xnv3[:, k, tsl], wv_buf[:, k * 256:(k + 1) * 256], o)
                           for k in range(KC)]
                    mm_group(bk, mms, extra_waits=pw)
                    pw = []
                    m = bank_read(SC, bk, lambda s, tc2=tc2, par=par, o=o: s.activation(
                        v_t[par][:, tc2 * 256:(tc2 + 1) * 256], o, AF.Silu))
                    pbi ^= 1
                    return m

                v_group(0)
                m_v[0] = v_group(1)
                for wbufs, dec, dst in [(wq_buf, qdec_bc, qt_t), (wk_buf, kdec_bc, kt_t)]:
                    for gl in range(G):
                        bk = banks[pbi]
                        o = bk.ap
                        mms = [(wbufs[gl][:, k * 128:(k + 1) * 128], xnv3[:, k, tok], o)
                               for k in range(KC)]
                        mm_group(bk, mms, extra_waits=pw)
                        pw = []
                        b6 = banks[6]
                        sw = [b6.rd] if b6.rd is not None else []
                        m_sil = bank_read(SC, bk, lambda s, o=o: s.activation(
                            b6.ap, o, AF.Silu), extra_waits=sw)
                        b6.wr = m_sil
                        bank_read(VE, b6, lambda v, gl=gl, par=par, dec=dec, dst=dst:
                                  v.tensor_tensor(
                                      dst[par][:, gl * 512:(gl + 1) * 512], b6.ap,
                                      dec[:, gl * 512:(gl + 1) * 512], ALU.mult))
                        pbi ^= 1
                v_group(2)
                m_v[1] = v_group(3)
                m_proj = (m_v, last("ve"))
                m_flq = VE(lambda v: v.engine_nop(), waits=[m_proj[1]])
                m_qt_dma[par] = GQ(lambda gq, g=g, bp=bp, par=par: gq.dma_start(
                    out=qt_hbm[g, bp], in_=qt_t[par]), waits=[m_flq])
                return m_proj

            def emit_sc_tr(bp, m_proj):
                par = bp % 2
                tb = banks[7]
                first_t = True
                for bi in range(2):
                    for gl in range(G):
                        sidx = bi * 2 + gl
                        bk = banks[2 + sidx]
                        ksl0 = gl * 512 + bi * 256
                        qsl = slice(gl * 512 + bi * 256, gl * 512 + (bi + 1) * 256)
                        mm_group(bk, [(kt_t[par][:, ksl0:ksl0 + 128], qt_t[par][:, qsl],
                                       bk.ap[:, 0:256])], extra_waits=[m_proj[1]])
                        mm_group(bk, [(kt_t[par][:, ksl0 + 128:ksl0 + 256],
                                       qt_t[par][:, qsl], bk.ap[:, 256:512])])
                        bank_read(VE, bk, lambda v, sidx=sidx, bk=bk: v.tensor_tensor(
                            scm_sb[:, sidx * 512:(sidx + 1) * 512], bk.ap, mask_b, ALU.mult),
                            extra_waits=ld0g)
                        tws = []
                        if first_t and tb.rd is not None:
                            tws.append(tb.rd)
                        first_t = False
                        TE(lambda t, ksl0=ksl0, par=par, sidx=sidx: t.transpose(
                            pt_b[:, sidx * 256:sidx * 256 + 128],
                            kt_t[par][:, ksl0:ksl0 + 128], ident_b), waits=tws, inc=None)
                        tb.wr = TE(lambda t, ksl0=ksl0, par=par, sidx=sidx: t.transpose(
                            pt_b[:, sidx * 256 + 128:sidx * 256 + 256],
                            kt_t[par][:, ksl0 + 128:ksl0 + 256], ident_b))
                bank_read(VE, tb, lambda v: v.tensor_copy(ktT_sb, pt_b))
                for gl in range(G):
                    h = g * G + gl
                    VE(lambda v, h=h, gl=gl: v.tensor_copy(
                        kv_bf[:, gl * 128:(gl + 1) * 128], kv[:, h * 128:(h + 1) * 128]),
                       waits=[m_kvup.get(h), m_kvbf_rd])
                return last("ve")

            def emit_rest(b, m_proj, m_sctr):
                nonlocal m_kvbf_rd
                par = (b // 2) % 2
                bi = b % 2
                m_scan_end = None
                for gl in range(G):
                    h = g * G + gl
                    sidx = bi * 2 + gl
                    qsl = slice(gl * 512 + bi * 256, gl * 512 + (bi + 1) * 256)
                    vsl = [slice((bi * 2 + j) * 256 + gl * 128,
                                 (bi * 2 + j) * 256 + gl * 128 + 128) for j in range(2)]
                    kvb = kv_bf[:, sidx * 128:(sidx + 1) * 128]
                    if bi == 1:
                        VE(lambda v, h=h, kvb=kvb: v.tensor_copy(
                            kvb, kv[:, h * 128:(h + 1) * 128]),
                           waits=[m_kvup.get(h), m_kvbf_rd])
                    m_cast = m_sctr if bi == 0 else last("ve")
                    bk = banks[2 + gl]
                    mms = []
                    if b > 0:
                        mms.append((kvb, qt_t[par][:, qsl], bk.ap[:, 0:256]))
                    mms.append((v_t[par][:, vsl[0]],
                                scm_sb[:, sidx * 512:sidx * 512 + 256], bk.ap[:, 0:256]))
                    mms.append((v_t[par][:, vsl[1]],
                                scm_sb[:, sidx * 512 + 256:sidx * 512 + 512], bk.ap[:, 0:256]))
                    mm_group(bk, mms, extra_waits=[m_sctr, m_cast, m_proj[0][bi]])
                    m_kvbf_rd = bk.wr
                    bank_read(SC, bk, lambda s, h=h, b=b, bk=bk: s.copy(
                        attn[:, h * T + b * 256:h * T + (b + 1) * 256], bk.ap[:, 0:256]))
                    ub = banks[4 + gl]
                    mm_group(ub, [
                        (ktT_sb[:, sidx * 256:sidx * 256 + 128], v_t[par][:, vsl[0]],
                         ub.ap[:, 0:128]),
                        (ktT_sb[:, sidx * 256 + 128:sidx * 256 + 256], v_t[par][:, vsl[1]],
                         ub.ap[:, 0:128])], extra_waits=[m_sctr])
                    m_add = bank_read(VE, ub, lambda v, h=h, ub=ub: v.tensor_tensor(
                        kvtmp, kv[:, h * 128:(h + 1) * 128], ub.ap[:, 0:128], ALU.add),
                        extra_waits=[m_kvup.get(h), m_cast])
                    VE(lambda v, h=h: v.tensor_scalar(kv[:, h * 128:(h + 1) * 128], kvtmp,
                                                      bd_bc[:, h:h + 1], None, ALU.mult),
                       waits=[m_add])
                    m_kvup[h] = last("ve")
                    m_scan_end = m_kvup[h]
                return m_scan_end

            mprojs = {0: emit_proj(0)}
            for bp in range(NB // 2):
                m_sctr = emit_sc_tr(bp, mprojs[bp])
                if bp + 1 < NB // 2:
                    mprojs[bp + 1] = emit_proj(bp + 1)
                emit_rest(2 * bp, mprojs[bp], m_sctr)
                m_tile_rd[bp % 2] = emit_rest(2 * bp + 1, mprojs[bp], m_sctr)
            m_w_rd = [banks[4].wr, banks[4].wr, banks[4].wr]
        m_attn_sc = last("sc")

        m_flkv = VE(lambda v: v.engine_nop(), waits=[m_kvup[NH - 1]])
        m_kvB = GQ(lambda gq: gq.dma_start(out=kvB.ap(), in_=kv), waits=[m_flkv])
        plan.on("gpsimd", lambda gq: gq.collective_compute(
            "AllGather", ALU.bypass, replica_groups=[list(range(NCORES))],
            ins=[kvB.ap().opt()], outs=[kvBg.ap().opt()]), waits=[m_kvB], inc="cc", inc_k=1)
        m_cc = ("cc", 1)
        VE(lambda v: v.memset(kv, 0.0), waits=[m_kvB])
        m_kv0b = last("ve")
        if "comb" in phases:
            emit_comb()
        tmpB = arena("tmpB", [128, NH * 128], F32, 90112)
        m_tmpB_rd = None
        for cp in range(7):
            m_t = GQ(lambda gq, cp=cp: gq.dma_start(
                out=tmpB, in_=kvBg.ap()[cp * 128:(cp + 1) * 128, :]),
                waits=[m_cc, m_tmpB_rd])
            for h in range(NH):
                VE(lambda v, cp=cp, h=h: v.tensor_scalar(
                    tmpB[:, h * 128:(h + 1) * 128], tmpB[:, h * 128:(h + 1) * 128],
                    lam_b[:, cp * 16 + h:cp * 16 + h + 1], None, ALU.mult),
                   waits=[m_t] if h == 0 else [])
            m_ts = last("ve")
            VE(lambda v: v.tensor_tensor(kv, kv, tmpB, ALU.add), waits=[m_ts, m_kv0b])
            m_tmpB_rd = last("ve")
        m_kvin = last("ve")

        qtt = [arena(f"qc{i}", [128, G * 512], BF16, 65536 + i * 2048) for i in range(2)]
        kvs_f = [arena(f"kvsf{i}", [128, 128], F32, 69632 + i * 512) for i in range(2)]
        kvs_b = [arena(f"kvsb{i}", [128, 128], BF16, 70656 + i * 256) for i in range(2)]
        m_qtt_rd = [None, None]
        m_corr_first = [m_attn_sc]
        for g in range(NG):
            for gl in range(G):
                h = g * G + gl
                VE(lambda v, gl=gl, h=h: v.tensor_copy(kvs_f[gl], kv[:, h * 128:(h + 1) * 128]),
                   waits=[m_kvin])
                VE(lambda v, gl=gl: v.tensor_copy(kvs_b[gl], kvs_f[gl]), waits=[last("ve")])
            m_kvs = last("ve")
            for bp in range(NB // 2):
                par = bp % 2
                m_q = GQ(lambda gq, g=g, bp=bp, par=par: gq.dma_start(
                    out=qtt[par], in_=qt_hbm[g, bp]),
                    waits=[m_qtt_rd[par]] + ([m_kvin] if (g == 0 and bp == 0) else []))
                for bi in range(2):
                    b = 2 * bp + bi
                    for gl in range(G):
                        h = g * G + gl
                        bk = banks[2 + gl]
                        o = bk.ap[:, 0:256]
                        qsl = slice(gl * 512 + bi * 256, gl * 512 + (bi + 1) * 256)
                        mm_group(bk, [(kvs_b[gl], qtt[par][:, qsl], o)],
                                 extra_waits=[m_q, m_kvs])
                        m_corr = bank_read(VE, bk, lambda v, h=h, b=b, o=o: v.tensor_tensor(
                            attn[:, h * T + b * 256:h * T + (b + 1) * 256],
                            attn[:, h * T + b * 256:h * T + (b + 1) * 256], o, ALU.add),
                            extra_waits=m_corr_first)
                        m_corr_first = []
                        m_qtt_rd[par] = m_corr
                        if b < NB - 1:
                            VE(lambda v, gl=gl, h=h: v.tensor_scalar(
                                kvs_f[gl], kvs_f[gl], bd_bc[:, h:h + 1], None, ALU.mult),
                               waits=[bk.wr])
                            VE(lambda v, gl=gl: v.tensor_copy(kvs_b[gl], kvs_f[gl]),
                               waits=[last("ve")])
                            m_kvs = last("ve")
        m_attn_final = last("ve")
        m_comb = comb_state["m_comb"]
        m_combT = comb_state["m_combT"]

    m_z = None
    if "ffn" in phases:
        attnv = attn.rearrange("p (h t) -> p h t", h=NH)
        sq = [arena(f"sq{i}", [128, D], BF16, 65536 + i * 4096) for i in range(2)]
        rstd_bc = arena("rstd_bc", [128, T], F32, 73728)
        wgate_buf = [arena(f"wgb{i}", [128, KC * 128], BF16, 90112 + i * 4096) for i in range(2)]
        sig = arena("sig", [128, 512], F32, 98304)
        t2 = arena("t2", [128, 512], F32, 100352)

        m_sq_rd = [None, None]
        for k in range(KC):
            par = k % 2
            SC(lambda s, k=k, par=par: s.activation(sq[par], attnv[:, k, :], AF.Square),
               waits=[m_attn_final, m_sq_rd[par]])
            m_sq = last("sc")
            for slab in range(4):
                bk = banks[slab]
                ws = [m_sq]
                if k == 0 and bk.rd is not None:
                    ws.append(bk.rd)
                def fn(t, par=par, slab=slab, k=k):
                    return t.matmul(pb[slab], ones_b, sq[par][:, slab * 512:(slab + 1) * 512],
                                    start=(k == 0), stop=(k == KC - 1))
                if slab == 3:
                    m_sq_rd[par] = TE(fn, waits=ws)
                else:
                    TE(fn, waits=ws, inc=None)
                if k == KC - 1:
                    bk.wr = m_sq_rd[par] if slab == 3 else None
        for slab in range(4):
            banks[slab].wr = m_sq_rd[(KC - 1) % 2]
        for slab in range(4):
            bk = banks[slab]
            m_lnr = bank_read(SC, bk, lambda s, slab=slab: s.activation(
                rstd_bc[:, slab * 512:(slab + 1) * 512], pb[slab], AF.Ln, scale=1.0 / D,
                bias=small1[:, 2:3]))
            SC(lambda s, slab=slab: s.activation(rstd_bc[:, slab * 512:(slab + 1) * 512],
                                                 rstd_bc[:, slab * 512:(slab + 1) * 512],
                                                 AF.Exp, scale=-0.5), waits=[m_lnr])
        m_rstd_bc = last("sc")

        m_wg_rd = [None, None]
        for o in range(KC):
            par = o % 2
            dd, ss = tiled(wgate_buf[par], dWgateT[o], KC)
            m_wl = GQ(lambda gq, dd=dd, ss=ss: gq.dma_start(out=dd, in_=ss),
                      waits=[m_wg_rd[par]])
            for slab in range(4):
                bk = banks[slab % 2]
                o_ap = bk.ap
                mms = [(wgate_buf[par][:, k * 128:(k + 1) * 128],
                        xnv3[:, k, slab * 512:(slab + 1) * 512], o_ap) for k in range(KC)]
                mm_group(bk, mms, extra_waits=[m_wl, m_rstd_bc])
                if slab == 3:
                    m_wg_rd[par] = bk.wr
                m_sg = bank_read(SC, bk, lambda s, o_ap=o_ap: s.activation(sig, o_ap, AF.Sigmoid),
                                 extra_waits=[("ve", plan.ctr.get("ve", 0))])
                VE(lambda v, o=o, slab=slab: v.scalar_tensor_tensor(
                    t2, sig, normw[:, o:o + 1], attnv[:, o, slab * 512:(slab + 1) * 512],
                    ALU.mult, ALU.mult), waits=[m_sg])
                VE(lambda v, o=o, slab=slab: v.tensor_tensor(
                    attnv[:, o, slab * 512:(slab + 1) * 512], t2,
                    rstd_bc[:, slab * 512:(slab + 1) * 512], ALU.mult), waits=[last("ve")])
        m_z = last("ve")

        m_pe1 = last("pe")
        wout_buf = [arena(f"wob{i}", [128, KC * 512], BF16, 65536 + i * 16384) for i in range(2)]
        seed_p = [arena(f"seedp{i}", [128, 1024], F32, 98304 + i * 4096) for i in range(2)]
        m_wo_rd = [None, None]
        m_a1_st = [None, None]
        for d in range(4):
            par = d % 2
            dd = wout_buf[par].rearrange("p (k n) -> p k n", k=KC)
            ss = dWoutT[:, d * 512:(d + 1) * 512].rearrange("(k p) n -> p k n", p=128)
            m_wl = GQ(lambda gq, dd=dd, ss=ss: gq.dma_start(out=dd, in_=ss),
                      waits=[m_wo_rd[par]] + ([m_z, m_pe1] if d < 2 else []))
            for t in range(16):
                bk = banks[t % 2]
                o_ap = bk.ap
                mms = [(attnv[:, o, t * 128:(t + 1) * 128],
                        wout_buf[par][:, o * 512:(o + 1) * 512], o_ap) for o in range(KC)]
                mm_group(bk, mms, extra_waits=[m_wl])
                if t >= 14:
                    m_wo_rd[par] = bk.wr
                tpar = (t // 2) % 2
                half = t % 2
                m_cp = bank_read(SC, bk, lambda s, tpar=tpar, half=half, o_ap=o_ap: s.copy(
                    seed_p[tpar][:, half * 512:(half + 1) * 512], o_ap),
                    extra_waits=[m_a1_st[tpar]] if half == 0 else [])
                if half == 1:
                    m_fl = SC(lambda s: s.copy(small1[0:1, 3:4], small1[0:1, 2:3]),
                              waits=[m_cp])
                    m_a1_st[tpar] = XQ(lambda sy, t=t, d=d, tpar=tpar: sy.dma_start(
                        out=acc_a[(t - 1) * 128:(t + 1) * 128,
                                  d * 512:(d + 1) * 512].rearrange("(g p) n -> p g n", p=128),
                        in_=seed_p[tpar].rearrange("p (g n) -> p g n", g=2)), waits=[m_fl])

        m_dump1 = None
        if dbg:
            m_dump1 = GQ(lambda gq: gq.dma_start(out=dbg_outs["acc1_dbg"], in_=acc_a),
                         waits=[m_a1_st[0], m_a1_st[1]])

        m_pe2 = last("pe")
        gsh = arena("gsh", [128, HC * T], BF16, 0)
        swg_buf = [arena(f"sgb{i}", [128, KC * 128], BF16, 45056 + i * 4096) for i in range(2)]
        swu_buf = [arena(f"sub{i}", [128, KC * 128], BF16, 53248 + i * 4096) for i in range(2)]
        swd_buf = [arena(f"sdb{i}", [128, HC * 512], BF16, 61440 + i * 11264) for i in range(2)]
        upc = arena("upc", [128, 512], BF16, 83968)
        sg_t = arena("sg_t", [128, 512], BF16, 84992)
        accin_p = [arena(f"acip{i}", [128, 1024], F32, 86016 + i * 4096) for i in range(2)]
        accout_p = [arena(f"acop{i}", [128, 1024], F32, 94208 + i * 4096) for i in range(2)]
        combc = arena("combc", [128, T], BF16, 102400)
        xin_p = arena("xin_p", [128, 1024], F32, 106496)
        m_acc_st = [None, m_a1_st[0]]
        m_aci_rd2 = [None, None]
        m_xin_rd = [None]
        m_sg_rd = [None, None]
        m_su_rd = [None, None]
        m_sd_rd = [None, None]
        m_aci_rd = [None, None]
        m_aco_st = [None, None]
        m_sgt_rd = None
        m_upc_rd = None

        def gup_preload(dG, dU, hp, fw=()):
            par = hp % 2
            dd, ss = tiled(swg_buf[par], dG[hp], KC)
            m = [GQ(lambda gq, dd=dd, ss=ss: gq.dma_start(out=dd, in_=ss),
                    waits=[m_sg_rd[par]] + list(fw))]
            dd, ss = tiled(swu_buf[par], dU[hp], KC)
            m.append(GQ(lambda gq, dd=dd, ss=ss: gq.dma_start(out=dd, in_=ss),
                        waits=[m_su_rd[par]]))
            return m

        def gup_phase(nchunks, dG, dU, gdst, e=None, with_comb=False, first_waits=(),
                      preloaded=None):
            nonlocal m_sgt_rd, m_upc_rd
            fw = list(first_waits)
            preloaded = dict(preloaded or {})
            for hp in range(nchunks):
                par = hp % 2
                if hp in preloaded:
                    m_wl = preloaded[hp] + fw
                    fw = []
                else:
                    m_wl = gup_preload(dG, dU, hp, fw)
                    fw = []
                for slab in range(4):
                    bg = banks[slab % 2]
                    bu = banks[2 + slab % 2]
                    og = bg.ap
                    ou = bu.ap
                    mg = [(swg_buf[par][:, k * 128:(k + 1) * 128],
                           xnv3[:, k, slab * 512:(slab + 1) * 512], og) for k in range(KC)]
                    mu = [(swu_buf[par][:, k * 128:(k + 1) * 128],
                           xnv3[:, k, slab * 512:(slab + 1) * 512], ou) for k in range(KC)]
                    mm_group(bg, mg, extra_waits=m_wl)
                    if slab == 3:
                        m_sg_rd[par] = bg.wr
                    mm_group(bu, mu)
                    if slab == 3:
                        m_su_rd[par] = bu.wr
                    m_sil = bank_read(SC, bg, lambda s, og=og: s.activation(sg_t, og, AF.Silu),
                                      extra_waits=[m_sgt_rd])
                    if with_comb:
                        m_u = bank_read(VE, bu, lambda v, ou=ou, slab=slab: v.tensor_tensor(
                            upc, ou, combc[:, slab * 512:(slab + 1) * 512], ALU.mult),
                            extra_waits=[m_upc_rd])
                        VE(lambda v, hp=hp, slab=slab, gdst=gdst: v.tensor_tensor(
                            gdst[:, hp * T + slab * 512:hp * T + (slab + 1) * 512],
                            sg_t, upc, ALU.mult), waits=[m_sil, m_u])
                        m_upc_rd = last("ve")
                        m_sgt_rd = m_upc_rd
                    else:
                        m_gv = bank_read(VE, bu, lambda v, ou=ou, hp=hp, slab=slab, gdst=gdst:
                                         v.tensor_tensor(
                                             gdst[:, hp * T + slab * 512:hp * T + (slab + 1) * 512],
                                             sg_t, ou, ALU.mult), extra_waits=[m_sil])
                        m_sgt_rd = m_gv
            return last("ve")

        def down_phase(nchunks, dD_slice_fn, gsrc, mode, m_gsrc=None, first_waits=(),
                       hook=None):
            gate = [m_gsrc]
            fw = list(first_waits)
            for d in range(4):
                par = d % 2
                src = dD_slice_fn(d)
                dd = swd_buf[par].rearrange("p (c n) -> p c n", c=nchunks)[:, 0:nchunks, :]
                ss = src.rearrange("(c p) n -> p c n", p=128)
                m_wl = GQ(lambda gq, dd=dd, ss=ss: gq.dma_start(out=dd, in_=ss),
                          waits=[m_sd_rd[par]])
                for t in range(16):
                    bk = banks[4 + t % 3]
                    o_ap = bk.ap
                    tpar = (t // 2) % 2
                    half = t % 2
                    if half == 0:
                        GQ(lambda gq, t=t, d=d, tpar=tpar: gq.dma_start(
                            out=accin_p[tpar].rearrange("p (g n) -> p g n", g=2),
                            in_=acc_a[t * 128:(t + 2) * 128,
                                      d * 512:(d + 1) * 512].rearrange("(g p) n -> p g n", p=128)),
                           waits=[m_aci_rd2[tpar]] + fw)
                        fw = []
                        m_aci_ld = last(GQ_POOL[(_qi["g"] - 1) % len(GQ_POOL)])
                        if mode == "final":
                            XQ(lambda sy, t=t, d=d: sy.dma_start(
                                out=xin_p.rearrange("p (g n) -> p g n", g=2),
                                in_=dx[t * 128:(t + 2) * 128,
                                       d * 512:(d + 1) * 512].rearrange("(g p) n -> p g n", p=128)),
                               waits=[m_xin_rd[0]])
                            m_xin_ld = last(XQ_POOL[(_qi["x"] - 1) % len(XQ_POOL)])
                    mms = [(gsrc[:, hp * T + t * 128:hp * T + (t + 1) * 128],
                            swd_buf[par][:, hp * 512:(hp + 1) * 512], o_ap)
                           for hp in range(nchunks)]
                    mm_group(bk, mms, extra_waits=[m_wl] + gate)
                    gate = []
                    if t == 15:
                        m_sd_rd[par] = bk.wr
                        if d == 2 and hook is not None:
                            hook()
                    hsl = slice(half * 512, (half + 1) * 512)
                    m_tt = bank_read(VE, bk, lambda v, tpar=tpar, hsl=hsl, o_ap=o_ap:
                                     v.tensor_tensor(accout_p[tpar][:, hsl], o_ap,
                                                     accin_p[tpar][:, hsl], ALU.add),
                                     extra_waits=[m_aci_ld] + ([m_acc_st[tpar]] if half == 0 else []))
                    if mode == "final":
                        m_tt = VE(lambda v, tpar=tpar, hsl=hsl: v.tensor_tensor(
                            accout_p[tpar][:, hsl], accout_p[tpar][:, hsl], xin_p[:, hsl],
                            ALU.add), waits=[m_tt, m_xin_ld])
                    if half == 1:
                        m_aci_rd2[tpar] = m_tt
                        if mode == "final":
                            m_xin_rd[0] = m_tt
                        m_fl = VE(lambda v: v.engine_nop(), waits=[m_tt])
                        dst = dout if mode == "final" else acc_a
                        XQ(lambda sy, t=t, d=d, tpar=tpar, dst=dst: sy.dma_start(
                            out=dst[(t - 1) * 128:(t + 1) * 128,
                                    d * 512:(d + 1) * 512].rearrange("(g p) n -> p g n", p=128),
                            in_=accout_p[tpar].rearrange("p (g n) -> p g n", g=2)),
                           waits=[m_fl])
                        m_acc_st[tpar] = last(XQ_POOL[(_qi["x"] - 1) % len(XQ_POOL)])

        m_gsh = gup_phase(HC, [dSwg[hp] for hp in range(HC)], [dSwu[hp] for hp in range(HC)],
                          gsh, first_waits=[m_pe2])
        if dbg:
            m_gsh = GQ(lambda gq: gq.dma_start(out=dbg_outs["gsh_dbg"], in_=gsh),
                       waits=[m_gsh])
        down_phase(HC, lambda d: dSwd[0:1408, d * 512:(d + 1) * 512], gsh, "accum",
                   m_gsrc=m_gsh, first_waits=[m_a1_st[0], m_a1_st[1], m_dump1])
        m_gsh = gup_phase(HC, [dSwg[HC + hp] for hp in range(HC)],
                          [dSwu[HC + hp] for hp in range(HC)], gsh)
        down_phase(HC, lambda d: dSwd[1408:2816, d * 512:(d + 1) * 512], gsh, "accum",
                   m_gsrc=m_gsh, first_waits=[m_acc_st[0], m_acc_st[1]])

        m_dump2 = None
        if dbg:
            m_dump2 = GQ(lambda gq: gq.dma_start(out=dbg_outs["acc2_dbg"], in_=acc_a),
                         waits=[m_acc_st[0], m_acc_st[1]])

        pre_g = {}
        for e in range(E):
            for slab in range(4):
                bk = banks[6]
                mm_group(bk, [(sel8_b[:, e * 128:(e + 1) * 128],
                               combT[:, slab * 512:(slab + 1) * 512], pb[6])],
                         extra_waits=[m_combT])
                bank_read(VE, bk, lambda v, slab=slab: v.tensor_copy(
                    combc[:, slab * 512:(slab + 1) * 512], pb[6]),
                    extra_waits=[m_upc_rd] + ([m_a1_st[1]] if e == 0 and slab == 0 else []))
            m_cbc = last("ve")
            m_upc_rd = m_cbc
            m_gc = gup_phase(HC, [dWg[e, hp] for hp in range(HC)],
                             [dWu[e, hp] for hp in range(HC)],
                             gsh, with_comb=True, first_waits=[m_cbc],
                             preloaded=pre_g.pop(e, None))
            is_last = (e == E - 1)
            if dbg and e == 0:
                m_gc = GQ(lambda gq: gq.dma_start(out=dbg_outs["gc_dbg"], in_=gsh),
                          waits=[m_gc])
            def prefetch_next(e=e):
                if e + 1 < E:
                    pre_g[e + 1] = {hp: gup_preload([dWg[e + 1, h2] for h2 in range(HC)],
                                                    [dWu[e + 1, h2] for h2 in range(HC)], hp)
                                    for hp in range(2)}
            down_phase(HC, lambda d, e=e: dWd[e][:, d * 512:(d + 1) * 512], gsh,
                       "final" if is_last else "accum", m_gsrc=m_gc,
                       first_waits=[m_acc_st[0], m_acc_st[1]] + ([m_dump2] if e == 0 else []),
                       hook=prefetch_next)

    if dbg:
        GQ(lambda g: g.dma_start(out=dbg_outs["xn_fm"], in_=xn), waits=[m_xn_done])
        if m_attn_final is not None and "ffn" not in phases:
            GQ(lambda g: g.dma_start(out=dbg_outs["attn_fm"], in_=attn), waits=[m_attn_final])
        if m_attn_final is not None:
            GQ(lambda g: g.dma_start(out=dbg_outs["kvin_dbg"], in_=kv), waits=[m_kvin])
        if m_z is not None:
            GQ(lambda g: g.dma_start(out=dbg_outs["z_fm"], in_=attn), waits=[m_z])
            GQ(lambda g: g.dma_start(out=dbg_outs["rstd_dbg"], in_=rstd_bc),
               waits=[last("sc")])
        if m_comb is not None:
            GQ(lambda g: g.dma_start(out=dbg_outs["comb_dbg"], in_=comb), waits=[m_comb])
            GQ(lambda g: g.dma_start(out=dbg_outs["small2_dbg"], in_=small2),
               waits=[m_comb, last("sc")])
        if "ffn" not in phases:
            XQ(lambda sy: sy.dma_start(out=dout, in_=dx))
    plan.on("sync", lambda sy: sy.nop(), waits=all_dma_done(), inc=None)

    import contextlib
    with contextlib.ExitStack() as es:
        sems = {s: es.enter_context(nc.semaphore(s)) for s in SEMS}
        with nc.Block() as block:
            def mk(ename):
                def body(eng):
                    for waits, fn, inc, k in plan.items[ename]:
                        for (s, v) in waits:
                            eng.wait_ge(sems[s], v)
                        instr = fn(eng)
                        if inc is not None:
                            instr.then_inc(sems[inc], k)
                return body
            block.sync(mk("sync"))
            block.tensor(mk("tensor"))
            block.vector(mk("vector"))
            block.scalar(mk("scalar"))
            block.gpsimd(mk("gpsimd"))
    return nc



def _in_maps(L0_attn_hidden_states, L0_attn_slope_rate, L0_attn_w_qkv, L0_attn_w_gate,
             L0_attn_w_out, L0_attn_norm_w, L0_ffn_gate_w, L0_ffn_w_gate, L0_ffn_w_up,
             L0_ffn_w_down, L0_ffn_sw_gate, L0_ffn_sw_up, L0_ffn_sw_down):
    shared = _prep_shared(L0_attn_w_qkv, L0_attn_w_gate, L0_attn_w_out, L0_ffn_gate_w,
                          L0_ffn_w_gate, L0_ffn_w_up, L0_ffn_w_down, L0_ffn_sw_gate,
                          L0_ffn_sw_up, L0_ffn_sw_down, L0_attn_norm_w)
    ins = []
    for c in range(NCORES):
        m = dict(shared)
        m.update(_prep_core(c, L0_attn_hidden_states, L0_attn_slope_rate))
        ins.append(m)
    return ins


def kernel(**inputs):
    in_maps = _in_maps(**inputs)
    nc = build_program(dbg=DBG)
    res = run_bass_kernel_spmd(nc, in_maps, core_ids=list(range(NCORES)))
    out = np.concatenate([r["out"] for r in res.results], axis=0).reshape(1, NCORES * T, D)
    if DBG:
        kernel.dbg_results = res.results
    return out.astype(np.float32)


# version 15
# speedup vs baseline: 1.0524x; 1.0524x over previous
import numpy as np
import concourse.bass as bass
import concourse.mybir as mybir
from concourse.bass_utils import run_bass_kernel_spmd

F32 = mybir.dt.float32
BF16 = mybir.dt.bfloat16
AF = mybir.ActivationFunctionType
ALU = mybir.AluOpType
AX = mybir.AxisListType

NCORES = 8
T = 2048
D = 2048
NH = 16
DH = 128
BLK = 256
NB = T // BLK
G = 2
NG = NH // G
E = 8
H = 1408
HC = H // 128
HS = 2816
HSC = HS // 128
KC = D // 128
LN_EPS = 1e-5
RMS_EPS = 1e-6

DBG = False



class Plan:
    ENGINES = ("sync", "tensor", "vector", "scalar", "gpsimd")

    def __init__(self):
        self.items = {e: [] for e in self.ENGINES}
        self.ctr = {}

    def on(self, eng, fn, waits=(), inc=None, inc_k=1):
        waits = [w for w in waits if w is not None]
        for s, v in waits:
            have = self.ctr.get(s, 0)
            assert v <= have, f"wait on {s}>={v} but only {have} planned"
        if inc is not None:
            self.ctr[inc] = self.ctr.get(inc, 0) + inc_k
        self.items[eng].append((waits, fn, inc, inc_k))
        return (inc, self.ctr[inc]) if inc is not None else None


class Bank:
    def __init__(self, ap):
        self.ap = ap
        self.wr = None
        self.rd = None



def _prep_shared(wqkv, wgate, wout, gate_w, ffn_wg, ffn_wu, ffn_wd, swg, swu, swd, norm_w):
    f32 = np.float32

    def lhsT_blocks(WT, nblk):
        Din, O = WT.shape
        assert O == nblk * 128
        return np.ascontiguousarray(WT.reshape(Din, nblk, 128).transpose(1, 0, 2))

    wqkvT = np.ascontiguousarray(np.asarray(wqkv, f32).T)
    Wq = lhsT_blocks(wqkvT[:, 0:2048], 16)
    Wk = lhsT_blocks(wqkvT[:, 2048:4096], 16)
    Wv = np.ascontiguousarray(
        wqkvT[:, 4096:6144].reshape(D, NG, G * 128).transpose(1, 0, 2))
    WgateT = lhsT_blocks(np.ascontiguousarray(np.asarray(wgate, f32).T), 16)
    WoutT = np.ascontiguousarray(np.asarray(wout, f32).T)
    Wg = np.ascontiguousarray(np.asarray(ffn_wg, f32).reshape(E, D, HC, 128).transpose(0, 2, 1, 3))
    Wu = np.ascontiguousarray(np.asarray(ffn_wu, f32).reshape(E, D, HC, 128).transpose(0, 2, 1, 3))
    Wd = np.ascontiguousarray(np.asarray(ffn_wd, f32))
    Swg = lhsT_blocks(np.ascontiguousarray(np.asarray(swg, f32)), HSC)
    Swu = lhsT_blocks(np.ascontiguousarray(np.asarray(swu, f32)), HSC)
    Swd = np.ascontiguousarray(np.asarray(swd, f32))
    gw_t = np.ascontiguousarray(
        np.asarray(gate_w, f32).reshape(16, 128, 8).transpose(1, 0, 2).reshape(128, 128))
    normw_t = np.ascontiguousarray(np.asarray(norm_w, f32).reshape(16, 128).T)

    ident = np.eye(128, dtype=f32)
    ident16 = np.eye(16, dtype=f32)
    ones16x128 = np.ones((16, 128), f32)
    ar16 = np.broadcast_to(np.arange(1, BLK + 1, dtype=f32)[None, :], (16, BLK)).copy()
    i_idx = np.arange(BLK)
    m0 = (i_idx[None, :] >= np.arange(0, 128)[:, None]).astype(f32)
    m1 = (i_idx[None, :] >= np.arange(128, 256)[:, None]).astype(f32)
    mask = np.ascontiguousarray(np.concatenate([m0, m1], axis=1))
    sel16 = np.zeros((16, 16 * 128), f32)
    for h in range(16):
        sel16[h, h * 128:(h + 1) * 128] = 1.0
    sel8 = np.zeros((8, 8 * 128), f32)
    for e in range(8):
        sel8[e, e * 128:(e + 1) * 128] = 1.0
    return dict(Wq=Wq, Wk=Wk, Wv=Wv, WgateT=WgateT, WoutT=WoutT, Wg=Wg, Wu=Wu, Wd=Wd,
                Swg=Swg, Swu=Swu, Swd=Swd, gw_t=gw_t, normw_t=normw_t, ident=ident,
                ident16=ident16, ones16x128=ones16x128, ar16=ar16, mask=mask,
                sel16=sel16, sel8=sel8)


def _prep_core(core, x_full, slope):
    f32 = np.float32
    xs = np.ascontiguousarray(np.asarray(x_full, f32).reshape(NCORES, T, D)[core])
    prevdec = np.full((16, 7), 1e7, f32)
    for cp in range(core):
        prevdec[:, cp] = 2048.0 * (core - 1 - cp)
    return dict(x=xs, prevdec=prevdec,
                slope=np.ascontiguousarray(np.asarray(slope, f32).reshape(16, 1)))



def build_program(dbg=False, phases=("p1", "comb", "attn", "ffn")):
    nc = bass.Bass(target_bir_lowering=False, debug=False, detect_race_conditions=False)

    dx = nc.dram_tensor("x", [T, D], F32, kind="ExternalInput").ap()
    dslope = nc.dram_tensor("slope", [16, 1], F32, kind="ExternalInput").ap()
    dprevdec = nc.dram_tensor("prevdec", [16, 7], F32, kind="ExternalInput").ap()
    dWq = nc.dram_tensor("Wq", [16, D, 128], F32, kind="ExternalInput").ap()
    dWk = nc.dram_tensor("Wk", [16, D, 128], F32, kind="ExternalInput").ap()
    dWv = nc.dram_tensor("Wv", [NG, D, G * 128], F32, kind="ExternalInput").ap()
    dWgateT = nc.dram_tensor("WgateT", [16, D, 128], F32, kind="ExternalInput").ap()
    dWoutT = nc.dram_tensor("WoutT", [D, D], F32, kind="ExternalInput").ap()
    dWg = nc.dram_tensor("Wg", [E, HC, D, 128], F32, kind="ExternalInput").ap()
    dWu = nc.dram_tensor("Wu", [E, HC, D, 128], F32, kind="ExternalInput").ap()
    dWd = nc.dram_tensor("Wd", [E, H, D], F32, kind="ExternalInput").ap()
    dSwg = nc.dram_tensor("Swg", [HSC, D, 128], F32, kind="ExternalInput").ap()
    dSwu = nc.dram_tensor("Swu", [HSC, D, 128], F32, kind="ExternalInput").ap()
    dSwd = nc.dram_tensor("Swd", [HS, D], F32, kind="ExternalInput").ap()
    dgw = nc.dram_tensor("gw_t", [128, 128], F32, kind="ExternalInput").ap()
    dnormw = nc.dram_tensor("normw_t", [128, 16], F32, kind="ExternalInput").ap()
    dident = nc.dram_tensor("ident", [128, 128], F32, kind="ExternalInput").ap()
    dident16 = nc.dram_tensor("ident16", [16, 16], F32, kind="ExternalInput").ap()
    dones16x128 = nc.dram_tensor("ones16x128", [16, 128], F32, kind="ExternalInput").ap()
    dar16 = nc.dram_tensor("ar16", [16, BLK], F32, kind="ExternalInput").ap()
    dmask = nc.dram_tensor("mask", [128, 512], F32, kind="ExternalInput").ap()
    dsel16 = nc.dram_tensor("sel16", [16, 2048], F32, kind="ExternalInput").ap()
    dsel8 = nc.dram_tensor("sel8", [8, 1024], F32, kind="ExternalInput").ap()
    dout = nc.dram_tensor("out", [T, D], F32, kind="ExternalOutput").ap()

    qt_hbm = nc.dram_tensor("qt_hbm", [NG, NB // 2, 128, G * 512], BF16).ap()
    kvB = nc.dram_tensor("kvB", [128, NH * 128], F32)
    kvBg = nc.dram_tensor("kvBg", [NCORES * 128, NH * 128], F32)
    acc_a = nc.dram_tensor("acc_a", [T, D], F32).ap()
    acc_b = nc.dram_tensor("acc_b", [T, D], F32).ap()

    dbg_outs = {}
    if dbg:
        for name, shape in [("xn_fm", [128, KC * T]), ("comb_dbg", [128, 128]),
                            ("attn_fm", [128, NH * T]), ("kv_dbg", [128, NH * 128]),
                            ("kvin_dbg", [128, NH * 128]), ("z_fm", [128, NH * T]),
                            ("rstd_dbg", [128, T]), ("small2_dbg", [128, 32]),
                            ("acc1_dbg", [T, D]), ("acc2_dbg", [T, D]),
                            ("gc_dbg", [128, HC * T]), ("gsh_dbg", [128, HC * T])]:
            dbg_outs[name] = nc.dram_tensor(name, shape, F32, kind="ExternalOutput").ap()

    SBUF_BASE = 20480
    off = [SBUF_BASE]

    def sb(name, shape, dtype, align=32):
        o = (off[0] + align - 1) // align * align
        t = nc.alloc_sbuf_tensor_at(name, shape, dtype, offset=o)
        itemsize = 2 if dtype == BF16 else 4
        off[0] = o + int(np.prod(shape[1:])) * itemsize
        return t.ap()

    xn = sb("xn", [128, KC * T], BF16)
    kv = sb("kv", [128, NH * 128], F32)
    combT = sb("combT", [8, T], BF16)
    comb = sb("comb", [128, 16 * 8], F32)
    normw = sb("normw", [128, 16], F32)
    ident_b = sb("ident_b", [128, 128], BF16)
    ident_f = sb("ident_f", [128, 128], F32)
    ident16 = sb("ident16_s", [16, 16], F32)
    ones16x128 = sb("ones16x128_s", [16, 128], F32)
    ones_b = sb("ones_b", [128, 128], BF16)
    mask_b = sb("mask_b", [128, 512], BF16)
    sel16_b = sb("sel16_b", [16, 2048], BF16)
    sel8_b = sb("sel8_b", [8, 1024], BF16)
    slope_c = sb("slope_c", [16, 1], F32)
    prevdec_s = sb("prevdec_s", [16, 7], F32)
    ar16_s = sb("ar16_s", [16, BLK], F32)
    sar = sb("sar", [16, BLK], F32)
    qdec_all = sb("qdec_all", [16, BLK], BF16)
    kdec_all = sb("kdec_all", [16, BLK], BF16)
    lpows = sb("lpows", [16, 7], F32)
    diag16 = sb("diag16", [16, 16], F32)
    bd_bc = sb("bd_bc", [128, 16], F32)
    lam_b = sb("lam_b", [128, 7 * 16], F32)
    kv_bf = sb("kv_bf", [128, 4 * 128], BF16)
    qdec_bc = sb("qdec_bc", [128, G * 2 * BLK], BF16)
    kdec_bc = sb("kdec_bc", [128, G * 2 * BLK], BF16)
    gw_b = sb("gw_b", [128, 128], BF16)
    small1 = sb("small1", [128, 16], F32)
    small2 = sb("small2", [128, 32], F32)
    small3 = sb("small3", [128, 32], F32)
    STATIC_END = off[0]
    ARENA = (STATIC_END + 127) // 128 * 128
    ARENA_END = 228352
    assert ARENA_END - ARENA >= 104448, (STATIC_END, ARENA)

    def arena(name, shape, dtype, at):
        o = ARENA + at
        itemsize = 2 if dtype == BF16 else 4
        end = o + int(np.prod(shape[1:])) * itemsize
        assert end <= ARENA_END, (name, end)
        assert o % 32 == 0, (name, o)
        return nc.alloc_sbuf_tensor_at(name, shape, dtype, offset=o).ap()

    pb = [nc.alloc_psum_tensor(f"pb{i}", [128, 512], F32).ap() for i in range(7)]
    pt_b = nc.alloc_psum_tensor("pt_b", [128, 1024], BF16).ap()
    banks = [Bank(p) for p in pb] + [Bank(pt_b)]

    plan = Plan()
    GQ_POOL = [f"wq{i}" for i in range(6)]
    XQ_POOL = [f"xq{i}" for i in range(4)]
    SEMS = GQ_POOL + XQ_POOL + ["pe", "sc", "ve", "gp", "cc"]
    _qi = {"g": 0, "x": 0}

    def S(eng_name, sem, k=1):
        def _on(fn, waits=(), inc=sem, inc_k=k):
            return plan.on(eng_name, fn, waits, inc, inc_k)
        return _on

    TE = S("tensor", "pe")
    VE = S("vector", "ve")
    SC = S("scalar", "sc")

    def GQ(fn, waits=()):
        s = GQ_POOL[_qi["g"] % len(GQ_POOL)]
        _qi["g"] += 1
        return plan.on("gpsimd", fn, waits, inc=s, inc_k=16)

    def XQ(fn, waits=()):
        s = XQ_POOL[_qi["x"] % len(XQ_POOL)]
        _qi["x"] += 1
        return plan.on("sync", fn, waits, inc=s, inc_k=16)

    def last(sem):
        return (sem, plan.ctr.get(sem, 0))

    def all_dma_done():
        return [(s, plan.ctr[s]) for s in GQ_POOL + XQ_POOL if plan.ctr.get(s, 0)]

    def mm_group(bank, mms, extra_waits=()):
        waits = list(extra_waits)
        if bank.rd is not None:
            waits.append(bank.rd)
        n = len(mms)
        for i, (lh, rh, o) in enumerate(mms):
            o = o if o is not None else bank.ap

            def fn(t, lh=lh, rh=rh, o=o, first=(i == 0), lastm=(i == n - 1)):
                return t.matmul(o, lh, rh, start=first, stop=lastm)
            if i == n - 1:
                bank.wr = TE(fn, waits=waits)
            else:
                TE(fn, waits=waits, inc=None)
            waits = []
        return bank.wr

    def tiled(dst2d, src2d, k):
        return (dst2d.rearrange("p (k m) -> p k m", k=k),
                src2d.rearrange("(k p) m -> p k m", p=128))

    def bank_read(eng_on, bank, fn, extra_waits=()):
        w = list(extra_waits)
        if bank.wr is not None:
            w.append(bank.wr)
        m = eng_on(fn, waits=w)
        bank.rd = m
        return m

    ld0 = []
    for dst, src in [(ident_f, dident), (ident16, dident16), (ones16x128, dones16x128),
                     (ar16_s, dar16), (slope_c, dslope),
                     (prevdec_s, dprevdec), (normw, dnormw)]:
        ld0.append(XQ(lambda sy, dst=dst, src=src: sy.dma_start(out=dst, in_=src)))
    ld0g = []
    for dst, src in [(ident_b, dident), (mask_b, dmask), (sel16_b, dsel16), (gw_b, dgw),
                     (sel8_b, dsel8)]:
        ld0g.append(GQ(lambda g, dst=dst, src=src: g.dma_start(out=dst, in_=src)))

    TE(lambda t: t.nop(), waits=ld0 + ld0g, inc=None)

    VE(lambda v: v.memset(ones_b, 1.0))
    VE(lambda v: v.memset(small1[:, 1:2], LN_EPS))
    VE(lambda v: v.memset(small1[:, 2:3], RMS_EPS))
    m_eps = last("ve")
    VE(lambda v: v.tensor_scalar(sar, ar16_s, slope_c, None, ALU.mult), waits=ld0)
    m_sar = last("ve")
    SC(lambda s: s.activation(qdec_all, sar, AF.Exp, scale=-1.0), waits=[m_sar])
    SC(lambda s: s.activation(kdec_all, sar, AF.Exp, scale=1.0))
    VE(lambda v: v.tensor_scalar(lpows, prevdec_s, slope_c, None, ALU.mult), waits=ld0)
    SC(lambda s: s.activation(lpows, lpows, AF.Exp, scale=-1.0), waits=[last("ve")])
    SC(lambda s: s.activation(small1[0:16, 0:1], slope_c, AF.Exp, scale=-256.0))
    m_exp = last("sc")

    VE(lambda v: v.tensor_scalar(diag16, ident16, small1[0:16, 0:1], None, ALU.mult),
       waits=[m_exp] + ld0)
    m_diag = last("ve")
    mm_group(banks[5], [(ones16x128, diag16, pb[5][:, 0:16])], extra_waits=[m_diag])
    bank_read(VE, banks[5], lambda v: v.tensor_copy(bd_bc, pb[5][:, 0:16]))
    for cp in range(7):
        VE(lambda v, cp=cp: v.tensor_scalar(diag16, ident16, lpows[:, cp:cp + 1], None, ALU.mult),
           waits=[m_exp])
        m_d = last("ve")
        mm_group(banks[5], [(ones16x128, diag16, pb[5][:, 0:16])], extra_waits=[m_d])
        bank_read(VE, banks[5],
                  lambda v, cp=cp: v.tensor_copy(lam_b[:, cp * 16:(cp + 1) * 16], pb[5][:, 0:16]))
    VE(lambda v: v.memset(kv, 0.0))

    lnst = [small2[:, 0:8], small2[:, 8:16]]
    xt = [arena("xt0", [128, D], F32, 32768), arena("xt1", [128, D], F32, 40960)]
    xc = arena("xc", [128, D], F32, 49152)
    xn_tm = [arena(f"xn_tm{i}", [128, D], BF16, 57344 + i * 4096) for i in range(2)]

    xt_rd = [None, None]
    m_xc_prev = None
    m_lnst_prev = [None, None]
    m_tr_prev = [None, None]
    m_xn_tc = []
    for tc in range(16):
        buf = tc % 2
        ln = lnst[buf]
        m_ld = XQ(lambda sy, tc=tc, buf=buf: sy.dma_start(out=xt[buf],
                                                          in_=dx[tc * 128:(tc + 1) * 128, :]),
                  waits=[xt_rd[buf]])
        VE(lambda v, buf=buf, ln=ln: v.reduce_sum(ln[:, 0:1], xt[buf], axis=AX.X),
           waits=[m_ld, m_lnst_prev[buf]])
        m_sum = last("ve")
        VE(lambda v, ln=ln: v.tensor_scalar(ln[:, 1:2], ln[:, 0:1], -1.0 / D, None, ALU.mult),
           waits=[m_sum])
        m_nm = last("ve")
        VE(lambda v, buf=buf, ln=ln: v.tensor_scalar(xc, xt[buf], ln[:, 1:2], None, ALU.add),
           waits=[m_xc_prev, m_nm])
        m_xc_w = last("ve")
        SC(lambda s, buf=buf, ln=ln: s.activation(xt[buf], xc, AF.Square, accum_out=ln[:, 2:3]),
           waits=[m_xc_w])
        xt_rd[buf] = last("sc")
        SC(lambda s, ln=ln: s.activation(ln[:, 3:4], ln[:, 2:3], AF.Ln, scale=1.0 / D,
                                         bias=small1[:, 1:2]), waits=[m_eps, xt_rd[buf]])
        m_ln = last("sc")
        SC(lambda s, ln=ln: s.activation(ln[:, 4:5], ln[:, 3:4], AF.Exp, scale=-0.5),
           waits=[m_ln])
        m_rstd = last("sc")
        m_lnst_prev[buf] = m_rstd
        VE(lambda v, buf=buf, ln=ln: v.tensor_scalar(xn_tm[buf], xc, ln[:, 4:5], None, ALU.mult),
           waits=[m_rstd, m_tr_prev[buf]])
        m_xn = last("ve")
        m_xc_prev = m_xn
        tb = banks[7]
        for half in range(2):
            mws = [m_xn]
            if tb.rd is not None:
                mws.append(tb.rd)
            for j in range(8):
                k = half * 8 + j

                def fn(t, k=k, j=j, buf=buf):
                    return t.transpose(pt_b[:, j * 128:(j + 1) * 128],
                                       xn_tm[buf][:, k * 128:(k + 1) * 128], ident_b)
                if j == 7:
                    tb.wr = TE(fn, waits=mws)
                else:
                    TE(fn, waits=mws, inc=None)
                mws = []

            def cp(s, tc=tc, half=half):
                dst = xn.rearrange("p (k t) -> p k t", k=KC)[:, half * 8:(half + 1) * 8,
                                                             tc * 128:(tc + 1) * 128]
                src = pt_b.rearrange("p (j t) -> p j t", j=8)
                return s.copy(dst, src)
            bank_read(SC, tb, cp)
        m_tr_prev[buf] = tb.wr
        m_xn_tc.append(last("sc"))
    m_xn_done = last("sc")

    comb_state = {"m_comb": None, "m_combT": None}

    def emit_comb():
        scores3 = small3[:, 0:8]
        msk3 = small3[:, 8:16]
        c2t3 = small3[:, 16:24]
        stat3 = small3[:, 24:28]
        m_comb = None
        xnv = xn.rearrange("p (k t) -> p k t", k=KC)
        for tc in range(16):
            bk = banks[6]
            mms = [(xnv[:, k, tc * 128:(tc + 1) * 128], gw_b[:, k * 8:(k + 1) * 8],
                    pb[6][:, 0:8]) for k in range(KC)]
            mm_group(bk, mms, extra_waits=[m_xn_done])
            bank_read(VE, bk, lambda v: v.reduce_max(stat3[:, 0:1], pb[6][:, 0:8], axis=AX.X),
                      extra_waits=[m_comb])
            VE(lambda v: v.tensor_scalar(stat3[:, 1:2], stat3[:, 0:1], -1.0, None, ALU.mult),
               waits=[last("ve")])
            m_negmax = last("ve")
            m_exp2 = bank_read(SC, bk,
                               lambda s: s.activation(scores3, pb[6][:, 0:8], AF.Exp,
                                                      bias=stat3[:, 1:2]),
                               extra_waits=[m_negmax])
            VE(lambda v: v.reduce_sum(stat3[:, 2:3], scores3, axis=AX.X), waits=[m_exp2])
            VE(lambda v: v.reciprocal(stat3[:, 3:4], stat3[:, 2:3]), waits=[last("ve")])
            VE(lambda v: v.tensor_scalar(scores3, scores3, stat3[:, 3:4], None, ALU.mult),
               waits=[last("ve")])
            VE(lambda v: v.reduce_max(stat3[:, 0:1], scores3, axis=AX.X), waits=[last("ve")])
            VE(lambda v, tc=tc: v.tensor_scalar(comb[:, tc * 8:(tc + 1) * 8], scores3,
                                                stat3[:, 0:1], None, ALU.is_ge), waits=[last("ve")])
            VE(lambda v, tc=tc: v.scalar_tensor_tensor(msk3, comb[:, tc * 8:(tc + 1) * 8],
                                                       -1e30, scores3, ALU.mult, ALU.add),
               waits=[last("ve")])
            VE(lambda v: v.reduce_max(stat3[:, 1:2], msk3, axis=AX.X), waits=[last("ve")])
            VE(lambda v: v.tensor_scalar(c2t3, msk3, stat3[:, 1:2], None, ALU.is_ge),
               waits=[last("ve")])
            VE(lambda v, tc=tc: v.tensor_add(comb[:, tc * 8:(tc + 1) * 8],
                                             comb[:, tc * 8:(tc + 1) * 8], c2t3),
               waits=[last("ve")])
            VE(lambda v, tc=tc: v.tensor_mul(comb[:, tc * 8:(tc + 1) * 8],
                                             comb[:, tc * 8:(tc + 1) * 8], scores3),
               waits=[last("ve")])
            m_comb = last("ve")
        for q in range(4):
            bk = banks[5]
            mws = [m_comb]
            if bk.rd is not None:
                mws.append(bk.rd)
            for j in range(4):
                tc = q * 4 + j

                def fn(t, tc=tc, j=j):
                    return t.transpose(pb[5][0:8, j * 128:(j + 1) * 128],
                                       comb[:, tc * 8:(tc + 1) * 8], ident_f)
                if j == 3:
                    bk.wr = TE(fn, waits=mws)
                else:
                    TE(fn, waits=mws, inc=None)
                mws = []
            bank_read(VE, bk,
                      lambda v, q=q: v.tensor_copy(combT[:, q * 512:(q + 1) * 512], pb[5][0:8, 0:512]))
        comb_state["m_comb"] = m_comb
        comb_state["m_combT"] = last("ve")

    if "comb" in phases and "attn" not in phases:
        emit_comb()
    m_comb = comb_state["m_comb"]
    m_combT = comb_state["m_combT"]

    m_attn_sc = None
    m_kvin = None
    m_attn_final = None
    if "attn" in phases:
        attn = arena("attn", [128, NH * T], BF16, 0)
        wq_buf = [arena(f"wqb{i}", [128, KC * 128], BF16, 65536 + i * 4096) for i in range(2)]
        wk_buf = [arena(f"wkb{i}", [128, KC * 128], BF16, 73728 + i * 4096) for i in range(2)]
        wv_buf = arena("wvb", [128, KC * 256], BF16, 81920)
        qt_t = [arena(f"qtt{i}", [128, G * 512], BF16, 90112 + i * 6144) for i in range(2)]
        kt_t = [arena(f"ktt{i}", [128, G * 512], BF16, 92160 + i * 6144) for i in range(2)]
        v_t = [arena(f"vtt{i}", [128, 4 * 256], BF16, 94208 + i * 6144) for i in range(2)]
        scm_sb = arena("scm_sb", [128, 4 * 512], BF16, 102400)
        ktT_sb = arena("ktT_sb", [128, 4 * 256], BF16, 106496)
        kvtmp = arena("kvtmp", [128, 128], F32, 108544)

        xnv3 = xn.rearrange("p (k t) -> p k t", k=KC)
        m_w_rd = [None, None, None]
        m_tile_rd = [None, None]
        m_qt_dma = [None, None]
        m_kvup = {}
        m_silv_rd = None
        m_kvbf_rd = None
        first_gq_wait = []

        for g in range(NG):
            gw_waits = list(first_gq_wait) + [m_w_rd[0], m_w_rd[1], m_w_rd[2]]
            first_gq_wait = []
            m_wld = []
            for gl in range(G):
                h = g * G + gl
                dq, sq_ = tiled(wq_buf[gl], dWq[h], KC)
                m_wld.append(GQ(lambda gq, dq=dq, sq_=sq_: gq.dma_start(out=dq, in_=sq_),
                                waits=gw_waits))
                gw_waits = []
                dk, sk_ = tiled(wk_buf[gl], dWk[h], KC)
                m_wld.append(GQ(lambda gq, dk=dk, sk_=sk_: gq.dma_start(out=dk, in_=sk_)))
            dv, sv_ = tiled(wv_buf, dWv[g], KC)
            m_wld.append(GQ(lambda gq, dv=dv, sv_=sv_: gq.dma_start(out=dv, in_=sv_)))

            for gl in range(G):
                h = g * G + gl
                for tab, dst in [(qdec_all, qdec_bc), (kdec_all, kdec_bc)]:
                    bw = [m_exp]
                    if banks[5].rd is not None:
                        bw.append(banks[5].rd)
                    TE(lambda t, h=h, tab=tab: t.matmul(
                        pb[5][:, 0:256], sel16_b[:, h * 128:(h + 1) * 128], tab,
                        start=True, stop=True), waits=bw, inc=None)
                    banks[5].wr = TE(lambda t, h=h, tab=tab: t.matmul(
                        pb[5][:, 256:512], sel16_b[:, h * 128:(h + 1) * 128], tab,
                        start=True, stop=True))
                    bank_read(VE, banks[5],
                              lambda v, gl=gl, dst=dst: v.tensor_copy(
                                  dst[:, gl * 512:(gl + 1) * 512], pb[5][:, 0:512]))
            m_dec = last("ve")

            def emit_proj(bp):
                par = bp % 2
                tok = slice(bp * 512, (bp + 1) * 512)
                pw = list(m_wld) + [m_dec, m_tile_rd[par], m_qt_dma[par],
                                    m_xn_tc[bp * 4 + 3]]
                pbi = 0
                m_v = [None, None]

                def v_group(tc2):
                    nonlocal pbi, pw
                    bk = banks[pbi]
                    o = bk.ap[:, 0:256]
                    tsl = slice((bp * 4 + tc2) * 128, (bp * 4 + tc2 + 1) * 128)
                    mms = [(xnv3[:, k, tsl], wv_buf[:, k * 256:(k + 1) * 256], o)
                           for k in range(KC)]
                    mm_group(bk, mms, extra_waits=pw)
                    pw = []
                    m = bank_read(SC, bk, lambda s, tc2=tc2, par=par, o=o: s.activation(
                        v_t[par][:, tc2 * 256:(tc2 + 1) * 256], o, AF.Silu))
                    pbi ^= 1
                    return m

                v_group(0)
                m_v[0] = v_group(1)
                for wbufs, dec, dst in [(wq_buf, qdec_bc, qt_t), (wk_buf, kdec_bc, kt_t)]:
                    for gl in range(G):
                        bk = banks[pbi]
                        o = bk.ap
                        mms = [(wbufs[gl][:, k * 128:(k + 1) * 128], xnv3[:, k, tok], o)
                               for k in range(KC)]
                        mm_group(bk, mms, extra_waits=pw)
                        pw = []
                        b6 = banks[6]
                        sw = [b6.rd] if b6.rd is not None else []
                        m_sil = bank_read(SC, bk, lambda s, o=o: s.activation(
                            b6.ap, o, AF.Silu), extra_waits=sw)
                        b6.wr = m_sil
                        bank_read(VE, b6, lambda v, gl=gl, par=par, dec=dec, dst=dst:
                                  v.tensor_tensor(
                                      dst[par][:, gl * 512:(gl + 1) * 512], b6.ap,
                                      dec[:, gl * 512:(gl + 1) * 512], ALU.mult))
                        pbi ^= 1
                v_group(2)
                m_v[1] = v_group(3)
                m_proj = (m_v, last("ve"))
                m_flq = VE(lambda v: v.engine_nop(), waits=[m_proj[1]])
                m_qt_dma[par] = GQ(lambda gq, g=g, bp=bp, par=par: gq.dma_start(
                    out=qt_hbm[g, bp], in_=qt_t[par]), waits=[m_flq])
                return m_proj

            def emit_sc_tr(bp, m_proj):
                par = bp % 2
                tb = banks[7]
                first_t = True
                for bi in range(2):
                    for gl in range(G):
                        sidx = bi * 2 + gl
                        bk = banks[2 + sidx]
                        ksl0 = gl * 512 + bi * 256
                        qsl = slice(gl * 512 + bi * 256, gl * 512 + (bi + 1) * 256)
                        mm_group(bk, [(kt_t[par][:, ksl0:ksl0 + 128], qt_t[par][:, qsl],
                                       bk.ap[:, 0:256])], extra_waits=[m_proj[1]])
                        mm_group(bk, [(kt_t[par][:, ksl0 + 128:ksl0 + 256],
                                       qt_t[par][:, qsl], bk.ap[:, 256:512])])
                        bank_read(VE, bk, lambda v, sidx=sidx, bk=bk: v.tensor_tensor(
                            scm_sb[:, sidx * 512:(sidx + 1) * 512], bk.ap, mask_b, ALU.mult),
                            extra_waits=ld0g)
                        tws = []
                        if first_t and tb.rd is not None:
                            tws.append(tb.rd)
                        first_t = False
                        TE(lambda t, ksl0=ksl0, par=par, sidx=sidx: t.transpose(
                            pt_b[:, sidx * 256:sidx * 256 + 128],
                            kt_t[par][:, ksl0:ksl0 + 128], ident_b), waits=tws, inc=None)
                        tb.wr = TE(lambda t, ksl0=ksl0, par=par, sidx=sidx: t.transpose(
                            pt_b[:, sidx * 256 + 128:sidx * 256 + 256],
                            kt_t[par][:, ksl0 + 128:ksl0 + 256], ident_b))
                bank_read(VE, tb, lambda v: v.tensor_copy(ktT_sb, pt_b))
                for gl in range(G):
                    h = g * G + gl
                    VE(lambda v, h=h, gl=gl: v.tensor_copy(
                        kv_bf[:, gl * 128:(gl + 1) * 128], kv[:, h * 128:(h + 1) * 128]),
                       waits=[m_kvup.get(h), m_kvbf_rd])
                return last("ve")

            def emit_rest(b, m_proj, m_sctr):
                nonlocal m_kvbf_rd
                par = (b // 2) % 2
                bi = b % 2
                m_scan_end = None
                for gl in range(G):
                    h = g * G + gl
                    sidx = bi * 2 + gl
                    qsl = slice(gl * 512 + bi * 256, gl * 512 + (bi + 1) * 256)
                    vsl = [slice((bi * 2 + j) * 256 + gl * 128,
                                 (bi * 2 + j) * 256 + gl * 128 + 128) for j in range(2)]
                    kvb = kv_bf[:, sidx * 128:(sidx + 1) * 128]
                    if bi == 1:
                        VE(lambda v, h=h, kvb=kvb: v.tensor_copy(
                            kvb, kv[:, h * 128:(h + 1) * 128]),
                           waits=[m_kvup.get(h), m_kvbf_rd])
                    m_cast = m_sctr if bi == 0 else last("ve")
                    bk = banks[2 + gl]
                    mms = []
                    if b > 0:
                        mms.append((kvb, qt_t[par][:, qsl], bk.ap[:, 0:256]))
                    mms.append((v_t[par][:, vsl[0]],
                                scm_sb[:, sidx * 512:sidx * 512 + 256], bk.ap[:, 0:256]))
                    mms.append((v_t[par][:, vsl[1]],
                                scm_sb[:, sidx * 512 + 256:sidx * 512 + 512], bk.ap[:, 0:256]))
                    mm_group(bk, mms, extra_waits=[m_sctr, m_cast, m_proj[0][bi]])
                    m_kvbf_rd = bk.wr
                    bank_read(SC, bk, lambda s, h=h, b=b, bk=bk: s.copy(
                        attn[:, h * T + b * 256:h * T + (b + 1) * 256], bk.ap[:, 0:256]))
                    ub = banks[4 + gl]
                    mm_group(ub, [
                        (ktT_sb[:, sidx * 256:sidx * 256 + 128], v_t[par][:, vsl[0]],
                         ub.ap[:, 0:128]),
                        (ktT_sb[:, sidx * 256 + 128:sidx * 256 + 256], v_t[par][:, vsl[1]],
                         ub.ap[:, 0:128])], extra_waits=[m_sctr])
                    m_add = bank_read(VE, ub, lambda v, h=h, ub=ub: v.tensor_tensor(
                        kvtmp, kv[:, h * 128:(h + 1) * 128], ub.ap[:, 0:128], ALU.add),
                        extra_waits=[m_kvup.get(h), m_cast])
                    VE(lambda v, h=h: v.tensor_scalar(kv[:, h * 128:(h + 1) * 128], kvtmp,
                                                      bd_bc[:, h:h + 1], None, ALU.mult),
                       waits=[m_add])
                    m_kvup[h] = last("ve")
                    m_scan_end = m_kvup[h]
                return m_scan_end

            mprojs = {0: emit_proj(0)}
            for bp in range(NB // 2):
                m_sctr = emit_sc_tr(bp, mprojs[bp])
                if bp + 1 < NB // 2:
                    mprojs[bp + 1] = emit_proj(bp + 1)
                emit_rest(2 * bp, mprojs[bp], m_sctr)
                m_tile_rd[bp % 2] = emit_rest(2 * bp + 1, mprojs[bp], m_sctr)
            m_w_rd = [banks[4].wr, banks[4].wr, banks[4].wr]
        m_attn_sc = last("sc")

        m_flkv = VE(lambda v: v.engine_nop(), waits=[m_kvup[NH - 1]])
        m_kvB = GQ(lambda gq: gq.dma_start(out=kvB.ap(), in_=kv), waits=[m_flkv])
        plan.on("gpsimd", lambda gq: gq.collective_compute(
            "AllGather", ALU.bypass, replica_groups=[list(range(NCORES))],
            ins=[kvB.ap().opt()], outs=[kvBg.ap().opt()]), waits=[m_kvB], inc="cc", inc_k=1)
        m_cc = ("cc", 1)
        VE(lambda v: v.memset(kv, 0.0), waits=[m_kvB])
        m_kv0b = last("ve")
        if "comb" in phases:
            emit_comb()
        tmpB = arena("tmpB", [128, NH * 128], F32, 90112)
        m_tmpB_rd = None
        for cp in range(7):
            m_t = GQ(lambda gq, cp=cp: gq.dma_start(
                out=tmpB, in_=kvBg.ap()[cp * 128:(cp + 1) * 128, :]),
                waits=[m_cc, m_tmpB_rd])
            for h in range(NH):
                VE(lambda v, cp=cp, h=h: v.tensor_scalar(
                    tmpB[:, h * 128:(h + 1) * 128], tmpB[:, h * 128:(h + 1) * 128],
                    lam_b[:, cp * 16 + h:cp * 16 + h + 1], None, ALU.mult),
                   waits=[m_t] if h == 0 else [])
            m_ts = last("ve")
            VE(lambda v: v.tensor_tensor(kv, kv, tmpB, ALU.add), waits=[m_ts, m_kv0b])
            m_tmpB_rd = last("ve")
        m_kvin = last("ve")

        qtt = [arena(f"qc{i}", [128, G * 512], BF16, 65536 + i * 2048) for i in range(2)]
        kvs_f = [arena(f"kvsf{i}", [128, 128], F32, 69632 + i * 512) for i in range(2)]
        kvs_b = [arena(f"kvsb{i}", [128, 128], BF16, 70656 + i * 256) for i in range(2)]
        m_qtt_rd = [None, None]
        m_corr_first = [m_attn_sc]
        for g in range(NG):
            for gl in range(G):
                h = g * G + gl
                VE(lambda v, gl=gl, h=h: v.tensor_copy(kvs_f[gl], kv[:, h * 128:(h + 1) * 128]),
                   waits=[m_kvin])
                VE(lambda v, gl=gl: v.tensor_copy(kvs_b[gl], kvs_f[gl]), waits=[last("ve")])
            m_kvs = last("ve")
            for bp in range(NB // 2):
                par = bp % 2
                m_q = GQ(lambda gq, g=g, bp=bp, par=par: gq.dma_start(
                    out=qtt[par], in_=qt_hbm[g, bp]),
                    waits=[m_qtt_rd[par]] + ([m_kvin] if (g == 0 and bp == 0) else []))
                for bi in range(2):
                    b = 2 * bp + bi
                    for gl in range(G):
                        h = g * G + gl
                        bk = banks[2 + gl]
                        o = bk.ap[:, 0:256]
                        qsl = slice(gl * 512 + bi * 256, gl * 512 + (bi + 1) * 256)
                        mm_group(bk, [(kvs_b[gl], qtt[par][:, qsl], o)],
                                 extra_waits=[m_q, m_kvs])
                        m_corr = bank_read(VE, bk, lambda v, h=h, b=b, o=o: v.tensor_tensor(
                            attn[:, h * T + b * 256:h * T + (b + 1) * 256],
                            attn[:, h * T + b * 256:h * T + (b + 1) * 256], o, ALU.add),
                            extra_waits=m_corr_first)
                        m_corr_first = []
                        m_qtt_rd[par] = m_corr
                        if b < NB - 1:
                            VE(lambda v, gl=gl, h=h: v.tensor_scalar(
                                kvs_f[gl], kvs_f[gl], bd_bc[:, h:h + 1], None, ALU.mult),
                               waits=[bk.wr])
                            VE(lambda v, gl=gl: v.tensor_copy(kvs_b[gl], kvs_f[gl]),
                               waits=[last("ve")])
                            m_kvs = last("ve")
        m_attn_final = last("ve")
        m_comb = comb_state["m_comb"]
        m_combT = comb_state["m_combT"]

    m_z = None
    if "ffn" in phases:
        attnv = attn.rearrange("p (h t) -> p h t", h=NH)
        sq = [arena(f"sq{i}", [128, D], BF16, 65536 + i * 4096) for i in range(2)]
        rstd_bc = arena("rstd_bc", [128, T], F32, 73728)
        wgate_buf = [arena(f"wgb{i}", [128, KC * 128], BF16, 90112 + i * 4096) for i in range(2)]
        sig = arena("sig", [128, 512], F32, 98304)
        t2 = arena("t2", [128, 512], F32, 100352)

        m_sq_rd = [None, None]
        for k in range(KC):
            par = k % 2
            SC(lambda s, k=k, par=par: s.activation(sq[par], attnv[:, k, :], AF.Square),
               waits=[m_attn_final, m_sq_rd[par]])
            m_sq = last("sc")
            for slab in range(4):
                bk = banks[slab]
                ws = [m_sq]
                if k == 0 and bk.rd is not None:
                    ws.append(bk.rd)
                def fn(t, par=par, slab=slab, k=k):
                    return t.matmul(pb[slab], ones_b, sq[par][:, slab * 512:(slab + 1) * 512],
                                    start=(k == 0), stop=(k == KC - 1))
                if slab == 3:
                    m_sq_rd[par] = TE(fn, waits=ws)
                else:
                    TE(fn, waits=ws, inc=None)
                if k == KC - 1:
                    bk.wr = m_sq_rd[par] if slab == 3 else None
        for slab in range(4):
            banks[slab].wr = m_sq_rd[(KC - 1) % 2]
        for slab in range(4):
            bk = banks[slab]
            m_lnr = bank_read(SC, bk, lambda s, slab=slab: s.activation(
                rstd_bc[:, slab * 512:(slab + 1) * 512], pb[slab], AF.Ln, scale=1.0 / D,
                bias=small1[:, 2:3]))
            SC(lambda s, slab=slab: s.activation(rstd_bc[:, slab * 512:(slab + 1) * 512],
                                                 rstd_bc[:, slab * 512:(slab + 1) * 512],
                                                 AF.Exp, scale=-0.5), waits=[m_lnr])
        m_rstd_bc = last("sc")

        m_wg_rd = [None, None]
        for o in range(KC):
            par = o % 2
            dd, ss = tiled(wgate_buf[par], dWgateT[o], KC)
            m_wl = GQ(lambda gq, dd=dd, ss=ss: gq.dma_start(out=dd, in_=ss),
                      waits=[m_wg_rd[par]])
            for slab in range(4):
                bk = banks[slab % 2]
                o_ap = bk.ap
                mms = [(wgate_buf[par][:, k * 128:(k + 1) * 128],
                        xnv3[:, k, slab * 512:(slab + 1) * 512], o_ap) for k in range(KC)]
                mm_group(bk, mms, extra_waits=[m_wl, m_rstd_bc])
                if slab == 3:
                    m_wg_rd[par] = bk.wr
                m_sg = bank_read(SC, bk, lambda s, o_ap=o_ap: s.activation(sig, o_ap, AF.Sigmoid),
                                 extra_waits=[("ve", plan.ctr.get("ve", 0))])
                VE(lambda v, o=o, slab=slab: v.scalar_tensor_tensor(
                    t2, sig, normw[:, o:o + 1], attnv[:, o, slab * 512:(slab + 1) * 512],
                    ALU.mult, ALU.mult), waits=[m_sg])
                VE(lambda v, o=o, slab=slab: v.tensor_tensor(
                    attnv[:, o, slab * 512:(slab + 1) * 512], t2,
                    rstd_bc[:, slab * 512:(slab + 1) * 512], ALU.mult), waits=[last("ve")])
        m_z = last("ve")

        m_pe1 = last("pe")
        wout_buf = [arena(f"wob{i}", [128, KC * 512], BF16, 65536 + i * 16384) for i in range(2)]
        seed_p = [arena(f"seedp{i}", [128, 1024], F32, 98304 + i * 4096) for i in range(2)]
        m_wo_rd = [None, None]
        m_a1_st = [None, None]
        for d in range(4):
            par = d % 2
            dd = wout_buf[par].rearrange("p (k n) -> p k n", k=KC)
            ss = dWoutT[:, d * 512:(d + 1) * 512].rearrange("(k p) n -> p k n", p=128)
            m_wl = GQ(lambda gq, dd=dd, ss=ss: gq.dma_start(out=dd, in_=ss),
                      waits=[m_wo_rd[par]] + ([m_z, m_pe1] if d < 2 else []))
            for t in range(16):
                bk = banks[t % 2]
                o_ap = bk.ap
                mms = [(attnv[:, o, t * 128:(t + 1) * 128],
                        wout_buf[par][:, o * 512:(o + 1) * 512], o_ap) for o in range(KC)]
                mm_group(bk, mms, extra_waits=[m_wl])
                if t >= 14:
                    m_wo_rd[par] = bk.wr
                tpar = (t // 2) % 2
                half = t % 2
                m_cp = bank_read(SC, bk, lambda s, tpar=tpar, half=half, o_ap=o_ap: s.copy(
                    seed_p[tpar][:, half * 512:(half + 1) * 512], o_ap),
                    extra_waits=[m_a1_st[tpar]] if half == 0 else [])
                if half == 1:
                    m_fl = SC(lambda s: s.copy(small1[0:1, 3:4], small1[0:1, 2:3]),
                              waits=[m_cp])
                    m_a1_st[tpar] = XQ(lambda sy, t=t, d=d, tpar=tpar: sy.dma_start(
                        out=acc_a[(t - 1) * 128:(t + 1) * 128,
                                  d * 512:(d + 1) * 512].rearrange("(g p) n -> p g n", p=128),
                        in_=seed_p[tpar].rearrange("p (g n) -> p g n", g=2)), waits=[m_fl])

        m_dump1 = None
        if dbg:
            m_dump1 = GQ(lambda gq: gq.dma_start(out=dbg_outs["acc1_dbg"], in_=acc_a),
                         waits=[m_a1_st[0], m_a1_st[1]])

        m_pe2 = last("pe")
        gsh = arena("gsh", [128, HC * T], BF16, 0)
        swg_buf = [arena(f"sgb{i}", [128, KC * 128], BF16, 45056 + i * 4096) for i in range(2)]
        swu_buf = [arena(f"sub{i}", [128, KC * 128], BF16, 53248 + i * 4096) for i in range(2)]
        swd_buf = [arena(f"sdb{i}", [128, HC * 512], BF16, 61440 + i * 11264) for i in range(2)]
        upc = arena("upc", [128, 512], BF16, 83968)
        sg_t = arena("sg_t", [128, 512], BF16, 84992)
        accin_p = [arena(f"acip{i}", [128, 1024], F32, 86016 + i * 4096) for i in range(2)]
        accout_p = [arena(f"acop{i}", [128, 1024], F32, 94208 + i * 4096) for i in range(2)]
        combc = arena("combc", [128, T], BF16, 102400)
        xin_p = arena("xin_p", [128, 1024], F32, 106496)
        m_acc_st = [None, m_a1_st[0]]
        m_aci_rd2 = [None, None]
        m_xin_rd = [None]
        m_sg_rd = [None, None]
        m_su_rd = [None, None]
        m_sd_rd = [None, None]
        m_aci_rd = [None, None]
        m_aco_st = [None, None]
        m_sgt_rd = None
        m_upc_rd = None

        def gup_preload(dG, dU, hp, fw=()):
            par = hp % 2
            dd, ss = tiled(swg_buf[par], dG[hp], KC)
            m = [GQ(lambda gq, dd=dd, ss=ss: gq.dma_start(out=dd, in_=ss),
                    waits=[m_sg_rd[par]] + list(fw))]
            dd, ss = tiled(swu_buf[par], dU[hp], KC)
            m.append(GQ(lambda gq, dd=dd, ss=ss: gq.dma_start(out=dd, in_=ss),
                        waits=[m_su_rd[par]]))
            return m

        def gup_phase(nchunks, dG, dU, gdst, e=None, with_comb=False, first_waits=(),
                      preloaded=None):
            nonlocal m_sgt_rd, m_upc_rd
            fw = list(first_waits)
            preloaded = dict(preloaded or {})
            for hp in range(nchunks):
                par = hp % 2
                if hp in preloaded:
                    m_wl = preloaded[hp] + fw
                    fw = []
                else:
                    m_wl = gup_preload(dG, dU, hp, fw)
                    fw = []
                for slab in range(4):
                    bg = banks[slab % 2]
                    bu = banks[2 + slab % 2]
                    og = bg.ap
                    ou = bu.ap
                    mg = [(swg_buf[par][:, k * 128:(k + 1) * 128],
                           xnv3[:, k, slab * 512:(slab + 1) * 512], og) for k in range(KC)]
                    mu = [(swu_buf[par][:, k * 128:(k + 1) * 128],
                           xnv3[:, k, slab * 512:(slab + 1) * 512], ou) for k in range(KC)]
                    mm_group(bg, mg, extra_waits=m_wl)
                    if slab == 3:
                        m_sg_rd[par] = bg.wr
                    mm_group(bu, mu)
                    if slab == 3:
                        m_su_rd[par] = bu.wr
                    m_sil = bank_read(SC, bg, lambda s, og=og: s.activation(sg_t, og, AF.Silu),
                                      extra_waits=[m_sgt_rd])
                    if with_comb:
                        m_u = bank_read(VE, bu, lambda v, ou=ou, slab=slab: v.tensor_tensor(
                            upc, ou, combc[:, slab * 512:(slab + 1) * 512], ALU.mult),
                            extra_waits=[m_upc_rd])
                        VE(lambda v, hp=hp, slab=slab, gdst=gdst: v.tensor_tensor(
                            gdst[:, hp * T + slab * 512:hp * T + (slab + 1) * 512],
                            sg_t, upc, ALU.mult), waits=[m_sil, m_u])
                        m_upc_rd = last("ve")
                        m_sgt_rd = m_upc_rd
                    else:
                        m_gv = bank_read(VE, bu, lambda v, ou=ou, hp=hp, slab=slab, gdst=gdst:
                                         v.tensor_tensor(
                                             gdst[:, hp * T + slab * 512:hp * T + (slab + 1) * 512],
                                             sg_t, ou, ALU.mult), extra_waits=[m_sil])
                        m_sgt_rd = m_gv
            return last("ve")

        def down_phase(nchunks, dD_slice_fn, gsrc, mode, m_gsrc=None, first_waits=(),
                       hook=None):
            gate = [m_gsrc]
            fw = list(first_waits)

            def swd_load(d):
                par = d % 2
                s2 = dD_slice_fn(d)
                dd = swd_buf[par].rearrange("p (c n) -> p c n", c=nchunks)[:, 0:nchunks, :]
                ss = s2.rearrange("(c p) n -> p c n", p=128)
                return GQ(lambda gq, dd=dd, ss=ss: gq.dma_start(out=dd, in_=ss),
                          waits=[m_sd_rd[par]])

            wl = {0: swd_load(0)}
            for d in range(4):
                par = d % 2
                m_wl = wl.pop(d)
                for t in range(16):
                    bk = banks[4 + t % 3]
                    o_ap = bk.ap
                    tpar = (t // 2) % 2
                    half = t % 2
                    if half == 0:
                        GQ(lambda gq, t=t, d=d, tpar=tpar: gq.dma_start(
                            out=accin_p[tpar].rearrange("p (g n) -> p g n", g=2),
                            in_=acc_a[t * 128:(t + 2) * 128,
                                      d * 512:(d + 1) * 512].rearrange("(g p) n -> p g n", p=128)),
                           waits=[m_aci_rd2[tpar]] + fw)
                        fw = []
                        m_aci_ld = last(GQ_POOL[(_qi["g"] - 1) % len(GQ_POOL)])
                        if mode == "final":
                            XQ(lambda sy, t=t, d=d: sy.dma_start(
                                out=xin_p.rearrange("p (g n) -> p g n", g=2),
                                in_=dx[t * 128:(t + 2) * 128,
                                       d * 512:(d + 1) * 512].rearrange("(g p) n -> p g n", p=128)),
                               waits=[m_xin_rd[0]])
                            m_xin_ld = last(XQ_POOL[(_qi["x"] - 1) % len(XQ_POOL)])
                    mms = [(gsrc[:, hp * T + t * 128:hp * T + (t + 1) * 128],
                            swd_buf[par][:, hp * 512:(hp + 1) * 512], o_ap)
                           for hp in range(nchunks)]
                    mm_group(bk, mms, extra_waits=[m_wl] + gate)
                    gate = []
                    if t == 7 and d + 1 < 4:
                        wl[d + 1] = swd_load(d + 1)
                    if t == 15:
                        m_sd_rd[par] = bk.wr
                        if d == 2 and hook is not None:
                            hook()
                    hsl = slice(half * 512, (half + 1) * 512)
                    m_tt = bank_read(VE, bk, lambda v, tpar=tpar, hsl=hsl, o_ap=o_ap:
                                     v.tensor_tensor(accout_p[tpar][:, hsl], o_ap,
                                                     accin_p[tpar][:, hsl], ALU.add),
                                     extra_waits=[m_aci_ld] + ([m_acc_st[tpar]] if half == 0 else []))
                    if mode == "final":
                        m_tt = VE(lambda v, tpar=tpar, hsl=hsl: v.tensor_tensor(
                            accout_p[tpar][:, hsl], accout_p[tpar][:, hsl], xin_p[:, hsl],
                            ALU.add), waits=[m_tt, m_xin_ld])
                    if half == 1:
                        m_aci_rd2[tpar] = m_tt
                        if mode == "final":
                            m_xin_rd[0] = m_tt
                        m_fl = VE(lambda v: v.engine_nop(), waits=[m_tt])
                        dst = dout if mode == "final" else acc_a
                        XQ(lambda sy, t=t, d=d, tpar=tpar, dst=dst: sy.dma_start(
                            out=dst[(t - 1) * 128:(t + 1) * 128,
                                    d * 512:(d + 1) * 512].rearrange("(g p) n -> p g n", p=128),
                            in_=accout_p[tpar].rearrange("p (g n) -> p g n", g=2)),
                           waits=[m_fl])
                        m_acc_st[tpar] = last(XQ_POOL[(_qi["x"] - 1) % len(XQ_POOL)])

        m_gsh = gup_phase(HC, [dSwg[hp] for hp in range(HC)], [dSwu[hp] for hp in range(HC)],
                          gsh, first_waits=[m_pe2])
        if dbg:
            m_gsh = GQ(lambda gq: gq.dma_start(out=dbg_outs["gsh_dbg"], in_=gsh),
                       waits=[m_gsh])
        down_phase(HC, lambda d: dSwd[0:1408, d * 512:(d + 1) * 512], gsh, "accum",
                   m_gsrc=m_gsh, first_waits=[m_a1_st[0], m_a1_st[1], m_dump1])
        m_gsh = gup_phase(HC, [dSwg[HC + hp] for hp in range(HC)],
                          [dSwu[HC + hp] for hp in range(HC)], gsh)
        down_phase(HC, lambda d: dSwd[1408:2816, d * 512:(d + 1) * 512], gsh, "accum",
                   m_gsrc=m_gsh, first_waits=[m_acc_st[0], m_acc_st[1]])

        m_dump2 = None
        if dbg:
            m_dump2 = GQ(lambda gq: gq.dma_start(out=dbg_outs["acc2_dbg"], in_=acc_a),
                         waits=[m_acc_st[0], m_acc_st[1]])

        pre_g = {}
        for e in range(E):
            for slab in range(4):
                bk = banks[6]
                mm_group(bk, [(sel8_b[:, e * 128:(e + 1) * 128],
                               combT[:, slab * 512:(slab + 1) * 512], pb[6])],
                         extra_waits=[m_combT])
                bank_read(VE, bk, lambda v, slab=slab: v.tensor_copy(
                    combc[:, slab * 512:(slab + 1) * 512], pb[6]),
                    extra_waits=[m_upc_rd] + ([m_a1_st[1]] if e == 0 and slab == 0 else []))
            m_cbc = last("ve")
            m_upc_rd = m_cbc
            m_gc = gup_phase(HC, [dWg[e, hp] for hp in range(HC)],
                             [dWu[e, hp] for hp in range(HC)],
                             gsh, with_comb=True, first_waits=[m_cbc],
                             preloaded=pre_g.pop(e, None))
            is_last = (e == E - 1)
            if dbg and e == 0:
                m_gc = GQ(lambda gq: gq.dma_start(out=dbg_outs["gc_dbg"], in_=gsh),
                          waits=[m_gc])
            def prefetch_next(e=e):
                if e + 1 < E:
                    pre_g[e + 1] = {hp: gup_preload([dWg[e + 1, h2] for h2 in range(HC)],
                                                    [dWu[e + 1, h2] for h2 in range(HC)], hp)
                                    for hp in range(2)}
            down_phase(HC, lambda d, e=e: dWd[e][:, d * 512:(d + 1) * 512], gsh,
                       "final" if is_last else "accum", m_gsrc=m_gc,
                       first_waits=[m_acc_st[0], m_acc_st[1]] + ([m_dump2] if e == 0 else []),
                       hook=prefetch_next)

    if dbg:
        GQ(lambda g: g.dma_start(out=dbg_outs["xn_fm"], in_=xn), waits=[m_xn_done])
        if m_attn_final is not None and "ffn" not in phases:
            GQ(lambda g: g.dma_start(out=dbg_outs["attn_fm"], in_=attn), waits=[m_attn_final])
        if m_attn_final is not None:
            GQ(lambda g: g.dma_start(out=dbg_outs["kvin_dbg"], in_=kv), waits=[m_kvin])
        if m_z is not None:
            GQ(lambda g: g.dma_start(out=dbg_outs["z_fm"], in_=attn), waits=[m_z])
            GQ(lambda g: g.dma_start(out=dbg_outs["rstd_dbg"], in_=rstd_bc),
               waits=[last("sc")])
        if m_comb is not None:
            GQ(lambda g: g.dma_start(out=dbg_outs["comb_dbg"], in_=comb), waits=[m_comb])
            GQ(lambda g: g.dma_start(out=dbg_outs["small2_dbg"], in_=small2),
               waits=[m_comb, last("sc")])
        if "ffn" not in phases:
            XQ(lambda sy: sy.dma_start(out=dout, in_=dx))
    plan.on("sync", lambda sy: sy.nop(), waits=all_dma_done(), inc=None)

    import contextlib
    with contextlib.ExitStack() as es:
        sems = {s: es.enter_context(nc.semaphore(s)) for s in SEMS}
        with nc.Block() as block:
            def mk(ename):
                def body(eng):
                    for waits, fn, inc, k in plan.items[ename]:
                        for (s, v) in waits:
                            eng.wait_ge(sems[s], v)
                        instr = fn(eng)
                        if inc is not None:
                            instr.then_inc(sems[inc], k)
                return body
            block.sync(mk("sync"))
            block.tensor(mk("tensor"))
            block.vector(mk("vector"))
            block.scalar(mk("scalar"))
            block.gpsimd(mk("gpsimd"))
    return nc



def _in_maps(L0_attn_hidden_states, L0_attn_slope_rate, L0_attn_w_qkv, L0_attn_w_gate,
             L0_attn_w_out, L0_attn_norm_w, L0_ffn_gate_w, L0_ffn_w_gate, L0_ffn_w_up,
             L0_ffn_w_down, L0_ffn_sw_gate, L0_ffn_sw_up, L0_ffn_sw_down):
    shared = _prep_shared(L0_attn_w_qkv, L0_attn_w_gate, L0_attn_w_out, L0_ffn_gate_w,
                          L0_ffn_w_gate, L0_ffn_w_up, L0_ffn_w_down, L0_ffn_sw_gate,
                          L0_ffn_sw_up, L0_ffn_sw_down, L0_attn_norm_w)
    ins = []
    for c in range(NCORES):
        m = dict(shared)
        m.update(_prep_core(c, L0_attn_hidden_states, L0_attn_slope_rate))
        ins.append(m)
    return ins


def kernel(**inputs):
    in_maps = _in_maps(**inputs)
    nc = build_program(dbg=DBG)
    res = run_bass_kernel_spmd(nc, in_maps, core_ids=list(range(NCORES)))
    out = np.concatenate([r["out"] for r in res.results], axis=0).reshape(1, NCORES * T, D)
    if DBG:
        kernel.dbg_results = res.results
    return out.astype(np.float32)


# version 16 (final)
# speedup vs baseline: 1.0687x; 1.0155x over previous
import numpy as np
import concourse.bass as bass
import concourse.mybir as mybir
from concourse.bass_utils import run_bass_kernel_spmd

F32 = mybir.dt.float32
BF16 = mybir.dt.bfloat16
AF = mybir.ActivationFunctionType
ALU = mybir.AluOpType
AX = mybir.AxisListType

NCORES = 8
T = 2048
D = 2048
NH = 16
DH = 128
BLK = 256
NB = T // BLK
G = 2
NG = NH // G
E = 8
H = 1408
HC = H // 128
HS = 2816
HSC = HS // 128
KC = D // 128
LN_EPS = 1e-5
RMS_EPS = 1e-6

DBG = False



class Plan:
    ENGINES = ("sync", "tensor", "vector", "scalar", "gpsimd")

    def __init__(self):
        self.items = {e: [] for e in self.ENGINES}
        self.ctr = {}

    def on(self, eng, fn, waits=(), inc=None, inc_k=1):
        waits = [w for w in waits if w is not None]
        for s, v in waits:
            have = self.ctr.get(s, 0)
            assert v <= have, f"wait on {s}>={v} but only {have} planned"
        if inc is not None:
            self.ctr[inc] = self.ctr.get(inc, 0) + inc_k
        self.items[eng].append((waits, fn, inc, inc_k))
        return (inc, self.ctr[inc]) if inc is not None else None


class Bank:
    def __init__(self, ap):
        self.ap = ap
        self.wr = None
        self.rd = None



def _prep_shared(wqkv, wgate, wout, gate_w, ffn_wg, ffn_wu, ffn_wd, swg, swu, swd, norm_w):
    f32 = np.float32

    def lhsT_blocks(WT, nblk):
        Din, O = WT.shape
        assert O == nblk * 128
        return np.ascontiguousarray(WT.reshape(Din, nblk, 128).transpose(1, 0, 2))

    wqkvT = np.ascontiguousarray(np.asarray(wqkv, f32).T)
    Wq = lhsT_blocks(wqkvT[:, 0:2048], 16)
    Wk = lhsT_blocks(wqkvT[:, 2048:4096], 16)
    Wv = np.ascontiguousarray(
        wqkvT[:, 4096:6144].reshape(D, NG, G * 128).transpose(1, 0, 2))
    WgateT = lhsT_blocks(np.ascontiguousarray(np.asarray(wgate, f32).T), 16)
    WoutT = np.ascontiguousarray(np.asarray(wout, f32).T)
    Wg = np.ascontiguousarray(np.asarray(ffn_wg, f32).reshape(E, D, HC, 128).transpose(0, 2, 1, 3))
    Wu = np.ascontiguousarray(np.asarray(ffn_wu, f32).reshape(E, D, HC, 128).transpose(0, 2, 1, 3))
    Wd = np.ascontiguousarray(np.asarray(ffn_wd, f32))
    Swg = lhsT_blocks(np.ascontiguousarray(np.asarray(swg, f32)), HSC)
    Swu = lhsT_blocks(np.ascontiguousarray(np.asarray(swu, f32)), HSC)
    Swd = np.ascontiguousarray(np.asarray(swd, f32))
    gw_t = np.ascontiguousarray(
        np.asarray(gate_w, f32).reshape(16, 128, 8).transpose(1, 0, 2).reshape(128, 128))
    normw_t = np.ascontiguousarray(np.asarray(norm_w, f32).reshape(16, 128).T)

    ident = np.eye(128, dtype=f32)
    ident16 = np.eye(16, dtype=f32)
    ones16x128 = np.ones((16, 128), f32)
    ar16 = np.broadcast_to(np.arange(1, BLK + 1, dtype=f32)[None, :], (16, BLK)).copy()
    i_idx = np.arange(BLK)
    m0 = (i_idx[None, :] >= np.arange(0, 128)[:, None]).astype(f32)
    m1 = (i_idx[None, :] >= np.arange(128, 256)[:, None]).astype(f32)
    mask = np.ascontiguousarray(np.concatenate([m0, m1], axis=1))
    sel16 = np.zeros((16, 16 * 128), f32)
    for h in range(16):
        sel16[h, h * 128:(h + 1) * 128] = 1.0
    sel8 = np.zeros((8, 8 * 128), f32)
    for e in range(8):
        sel8[e, e * 128:(e + 1) * 128] = 1.0
    return dict(Wq=Wq, Wk=Wk, Wv=Wv, WgateT=WgateT, WoutT=WoutT, Wg=Wg, Wu=Wu, Wd=Wd,
                Swg=Swg, Swu=Swu, Swd=Swd, gw_t=gw_t, normw_t=normw_t, ident=ident,
                ident16=ident16, ones16x128=ones16x128, ar16=ar16, mask=mask,
                sel16=sel16, sel8=sel8)


def _prep_core(core, x_full, slope):
    f32 = np.float32
    xs = np.ascontiguousarray(np.asarray(x_full, f32).reshape(NCORES, T, D)[core])
    prevdec = np.full((16, 7), 1e7, f32)
    for cp in range(core):
        prevdec[:, cp] = 2048.0 * (core - 1 - cp)
    return dict(x=xs, prevdec=prevdec,
                slope=np.ascontiguousarray(np.asarray(slope, f32).reshape(16, 1)))



def build_program(dbg=False, phases=("p1", "comb", "attn", "ffn")):
    nc = bass.Bass(target_bir_lowering=False, debug=False, detect_race_conditions=False)

    dx = nc.dram_tensor("x", [T, D], F32, kind="ExternalInput").ap()
    dslope = nc.dram_tensor("slope", [16, 1], F32, kind="ExternalInput").ap()
    dprevdec = nc.dram_tensor("prevdec", [16, 7], F32, kind="ExternalInput").ap()
    dWq = nc.dram_tensor("Wq", [16, D, 128], F32, kind="ExternalInput").ap()
    dWk = nc.dram_tensor("Wk", [16, D, 128], F32, kind="ExternalInput").ap()
    dWv = nc.dram_tensor("Wv", [NG, D, G * 128], F32, kind="ExternalInput").ap()
    dWgateT = nc.dram_tensor("WgateT", [16, D, 128], F32, kind="ExternalInput").ap()
    dWoutT = nc.dram_tensor("WoutT", [D, D], F32, kind="ExternalInput").ap()
    dWg = nc.dram_tensor("Wg", [E, HC, D, 128], F32, kind="ExternalInput").ap()
    dWu = nc.dram_tensor("Wu", [E, HC, D, 128], F32, kind="ExternalInput").ap()
    dWd = nc.dram_tensor("Wd", [E, H, D], F32, kind="ExternalInput").ap()
    dSwg = nc.dram_tensor("Swg", [HSC, D, 128], F32, kind="ExternalInput").ap()
    dSwu = nc.dram_tensor("Swu", [HSC, D, 128], F32, kind="ExternalInput").ap()
    dSwd = nc.dram_tensor("Swd", [HS, D], F32, kind="ExternalInput").ap()
    dgw = nc.dram_tensor("gw_t", [128, 128], F32, kind="ExternalInput").ap()
    dnormw = nc.dram_tensor("normw_t", [128, 16], F32, kind="ExternalInput").ap()
    dident = nc.dram_tensor("ident", [128, 128], F32, kind="ExternalInput").ap()
    dident16 = nc.dram_tensor("ident16", [16, 16], F32, kind="ExternalInput").ap()
    dones16x128 = nc.dram_tensor("ones16x128", [16, 128], F32, kind="ExternalInput").ap()
    dar16 = nc.dram_tensor("ar16", [16, BLK], F32, kind="ExternalInput").ap()
    dmask = nc.dram_tensor("mask", [128, 512], F32, kind="ExternalInput").ap()
    dsel16 = nc.dram_tensor("sel16", [16, 2048], F32, kind="ExternalInput").ap()
    dsel8 = nc.dram_tensor("sel8", [8, 1024], F32, kind="ExternalInput").ap()
    dout = nc.dram_tensor("out", [T, D], F32, kind="ExternalOutput").ap()

    qt_hbm = nc.dram_tensor("qt_hbm", [NG, NB // 2, 128, G * 512], BF16).ap()
    kvB = nc.dram_tensor("kvB", [128, NH * 128], F32)
    kvBg = nc.dram_tensor("kvBg", [NCORES * 128, NH * 128], F32)
    acc_a = nc.dram_tensor("acc_a", [T, D], F32).ap()
    acc_b = nc.dram_tensor("acc_b", [T, D], F32).ap()

    dbg_outs = {}
    if dbg:
        for name, shape in [("xn_fm", [128, KC * T]), ("comb_dbg", [128, 128]),
                            ("attn_fm", [128, NH * T]), ("kv_dbg", [128, NH * 128]),
                            ("kvin_dbg", [128, NH * 128]), ("z_fm", [128, NH * T]),
                            ("rstd_dbg", [128, T]), ("small2_dbg", [128, 32]),
                            ("acc1_dbg", [T, D]), ("acc2_dbg", [T, D]),
                            ("gc_dbg", [128, HC * T]), ("gsh_dbg", [128, HC * T])]:
            dbg_outs[name] = nc.dram_tensor(name, shape, F32, kind="ExternalOutput").ap()

    SBUF_BASE = 20480
    off = [SBUF_BASE]

    def sb(name, shape, dtype, align=32):
        o = (off[0] + align - 1) // align * align
        t = nc.alloc_sbuf_tensor_at(name, shape, dtype, offset=o)
        itemsize = 2 if dtype == BF16 else 4
        off[0] = o + int(np.prod(shape[1:])) * itemsize
        return t.ap()

    xn = sb("xn", [128, KC * T], BF16)
    kv = sb("kv", [128, NH * 128], F32)
    combT = sb("combT", [8, T], BF16)
    comb = sb("comb", [128, 16 * 8], F32)
    normw = sb("normw", [128, 16], F32)
    ident_b = sb("ident_b", [128, 128], BF16)
    ident_f = sb("ident_f", [128, 128], F32)
    ident16 = sb("ident16_s", [16, 16], F32)
    ones16x128 = sb("ones16x128_s", [16, 128], F32)
    ones_b = sb("ones_b", [128, 128], BF16)
    mask_b = sb("mask_b", [128, 512], BF16)
    sel16_b = sb("sel16_b", [16, 2048], BF16)
    sel8_b = sb("sel8_b", [8, 1024], BF16)
    slope_c = sb("slope_c", [16, 1], F32)
    prevdec_s = sb("prevdec_s", [16, 7], F32)
    ar16_s = sb("ar16_s", [16, BLK], F32)
    sar = sb("sar", [16, BLK], F32)
    qdec_all = sb("qdec_all", [16, BLK], BF16)
    kdec_all = sb("kdec_all", [16, BLK], BF16)
    lpows = sb("lpows", [16, 7], F32)
    diag16 = sb("diag16", [16, 16], F32)
    bd_bc = sb("bd_bc", [128, 16], F32)
    lam_b = sb("lam_b", [128, 7 * 16], F32)
    kv_bf = sb("kv_bf", [128, 4 * 128], BF16)
    qdec_bc = sb("qdec_bc", [128, G * 2 * BLK], BF16)
    kdec_bc = sb("kdec_bc", [128, G * 2 * BLK], BF16)
    gw_b = sb("gw_b", [128, 128], BF16)
    small1 = sb("small1", [128, 16], F32)
    small2 = sb("small2", [128, 32], F32)
    small3 = sb("small3", [128, 32], F32)
    STATIC_END = off[0]
    ARENA = (STATIC_END + 127) // 128 * 128
    ARENA_END = 228352
    assert ARENA_END - ARENA >= 104448, (STATIC_END, ARENA)

    def arena(name, shape, dtype, at):
        o = ARENA + at
        itemsize = 2 if dtype == BF16 else 4
        end = o + int(np.prod(shape[1:])) * itemsize
        assert end <= ARENA_END, (name, end)
        assert o % 32 == 0, (name, o)
        return nc.alloc_sbuf_tensor_at(name, shape, dtype, offset=o).ap()

    pb = [nc.alloc_psum_tensor(f"pb{i}", [128, 512], F32).ap() for i in range(7)]
    pt_b = nc.alloc_psum_tensor("pt_b", [128, 1024], BF16).ap()
    banks = [Bank(p) for p in pb] + [Bank(pt_b)]

    plan = Plan()
    GQ_POOL = [f"wq{i}" for i in range(6)]
    XQ_POOL = [f"xq{i}" for i in range(4)]
    SEMS = GQ_POOL + XQ_POOL + ["pe", "sc", "ve", "gp", "cc"]
    _qi = {"g": 0, "x": 0}

    def S(eng_name, sem, k=1):
        def _on(fn, waits=(), inc=sem, inc_k=k):
            return plan.on(eng_name, fn, waits, inc, inc_k)
        return _on

    TE = S("tensor", "pe")
    VE = S("vector", "ve")
    SC = S("scalar", "sc")

    def GQ(fn, waits=()):
        s = GQ_POOL[_qi["g"] % len(GQ_POOL)]
        _qi["g"] += 1
        return plan.on("gpsimd", fn, waits, inc=s, inc_k=16)

    def XQ(fn, waits=()):
        s = XQ_POOL[_qi["x"] % len(XQ_POOL)]
        _qi["x"] += 1
        return plan.on("sync", fn, waits, inc=s, inc_k=16)

    def last(sem):
        return (sem, plan.ctr.get(sem, 0))

    def all_dma_done():
        return [(s, plan.ctr[s]) for s in GQ_POOL + XQ_POOL if plan.ctr.get(s, 0)]

    def mm_group(bank, mms, extra_waits=()):
        waits = list(extra_waits)
        if bank.rd is not None:
            waits.append(bank.rd)
        n = len(mms)
        for i, (lh, rh, o) in enumerate(mms):
            o = o if o is not None else bank.ap

            def fn(t, lh=lh, rh=rh, o=o, first=(i == 0), lastm=(i == n - 1)):
                return t.matmul(o, lh, rh, start=first, stop=lastm)
            if i == n - 1:
                bank.wr = TE(fn, waits=waits)
            else:
                TE(fn, waits=waits, inc=None)
            waits = []
        return bank.wr

    def tiled(dst2d, src2d, k):
        return (dst2d.rearrange("p (k m) -> p k m", k=k),
                src2d.rearrange("(k p) m -> p k m", p=128))

    def bank_read(eng_on, bank, fn, extra_waits=()):
        w = list(extra_waits)
        if bank.wr is not None:
            w.append(bank.wr)
        m = eng_on(fn, waits=w)
        bank.rd = m
        return m

    ld0 = []
    for dst, src in [(ident_f, dident), (ident16, dident16), (ones16x128, dones16x128),
                     (ar16_s, dar16), (slope_c, dslope),
                     (prevdec_s, dprevdec), (normw, dnormw)]:
        ld0.append(XQ(lambda sy, dst=dst, src=src: sy.dma_start(out=dst, in_=src)))
    ld0g = []
    for dst, src in [(ident_b, dident), (mask_b, dmask), (sel16_b, dsel16), (gw_b, dgw),
                     (sel8_b, dsel8)]:
        ld0g.append(GQ(lambda g, dst=dst, src=src: g.dma_start(out=dst, in_=src)))

    TE(lambda t: t.nop(), waits=ld0 + ld0g, inc=None)

    VE(lambda v: v.memset(ones_b, 1.0))
    VE(lambda v: v.memset(small1[:, 1:2], LN_EPS))
    VE(lambda v: v.memset(small1[:, 2:3], RMS_EPS))
    m_eps = last("ve")
    VE(lambda v: v.tensor_scalar(sar, ar16_s, slope_c, None, ALU.mult), waits=ld0)
    m_sar = last("ve")
    SC(lambda s: s.activation(qdec_all, sar, AF.Exp, scale=-1.0), waits=[m_sar])
    SC(lambda s: s.activation(kdec_all, sar, AF.Exp, scale=1.0))
    VE(lambda v: v.tensor_scalar(lpows, prevdec_s, slope_c, None, ALU.mult), waits=ld0)
    SC(lambda s: s.activation(lpows, lpows, AF.Exp, scale=-1.0), waits=[last("ve")])
    SC(lambda s: s.activation(small1[0:16, 0:1], slope_c, AF.Exp, scale=-256.0))
    m_exp = last("sc")

    VE(lambda v: v.tensor_scalar(diag16, ident16, small1[0:16, 0:1], None, ALU.mult),
       waits=[m_exp] + ld0)
    m_diag = last("ve")
    mm_group(banks[5], [(ones16x128, diag16, pb[5][:, 0:16])], extra_waits=[m_diag])
    bank_read(VE, banks[5], lambda v: v.tensor_copy(bd_bc, pb[5][:, 0:16]))
    for cp in range(7):
        VE(lambda v, cp=cp: v.tensor_scalar(diag16, ident16, lpows[:, cp:cp + 1], None, ALU.mult),
           waits=[m_exp])
        m_d = last("ve")
        mm_group(banks[5], [(ones16x128, diag16, pb[5][:, 0:16])], extra_waits=[m_d])
        bank_read(VE, banks[5],
                  lambda v, cp=cp: v.tensor_copy(lam_b[:, cp * 16:(cp + 1) * 16], pb[5][:, 0:16]))
    VE(lambda v: v.memset(kv, 0.0))

    lnst = [small2[:, 0:8], small2[:, 8:16]]
    xt = [arena("xt0", [128, D], F32, 32768), arena("xt1", [128, D], F32, 40960)]
    xc = arena("xc", [128, D], F32, 49152)
    xn_tm = [arena(f"xn_tm{i}", [128, D], BF16, 57344 + i * 4096) for i in range(2)]

    xt_rd = [None, None]
    m_xc_prev = None
    m_lnst_prev = [None, None]
    m_tr_prev = [None, None]
    m_xn_tc = []
    for tc in range(16):
        buf = tc % 2
        ln = lnst[buf]
        m_ld = XQ(lambda sy, tc=tc, buf=buf: sy.dma_start(out=xt[buf],
                                                          in_=dx[tc * 128:(tc + 1) * 128, :]),
                  waits=[xt_rd[buf]])
        VE(lambda v, buf=buf, ln=ln: v.reduce_sum(ln[:, 0:1], xt[buf], axis=AX.X),
           waits=[m_ld, m_lnst_prev[buf]])
        m_sum = last("ve")
        VE(lambda v, ln=ln: v.tensor_scalar(ln[:, 1:2], ln[:, 0:1], -1.0 / D, None, ALU.mult),
           waits=[m_sum])
        m_nm = last("ve")
        VE(lambda v, buf=buf, ln=ln: v.tensor_scalar(xc, xt[buf], ln[:, 1:2], None, ALU.add),
           waits=[m_xc_prev, m_nm])
        m_xc_w = last("ve")
        SC(lambda s, buf=buf, ln=ln: s.activation(xt[buf], xc, AF.Square, accum_out=ln[:, 2:3]),
           waits=[m_xc_w])
        xt_rd[buf] = last("sc")
        SC(lambda s, ln=ln: s.activation(ln[:, 3:4], ln[:, 2:3], AF.Ln, scale=1.0 / D,
                                         bias=small1[:, 1:2]), waits=[m_eps, xt_rd[buf]])
        m_ln = last("sc")
        SC(lambda s, ln=ln: s.activation(ln[:, 4:5], ln[:, 3:4], AF.Exp, scale=-0.5),
           waits=[m_ln])
        m_rstd = last("sc")
        m_lnst_prev[buf] = m_rstd
        VE(lambda v, buf=buf, ln=ln: v.tensor_scalar(xn_tm[buf], xc, ln[:, 4:5], None, ALU.mult),
           waits=[m_rstd, m_tr_prev[buf]])
        m_xn = last("ve")
        m_xc_prev = m_xn
        tb = banks[7]
        for half in range(2):
            mws = [m_xn]
            if tb.rd is not None:
                mws.append(tb.rd)
            for j in range(8):
                k = half * 8 + j

                def fn(t, k=k, j=j, buf=buf):
                    return t.transpose(pt_b[:, j * 128:(j + 1) * 128],
                                       xn_tm[buf][:, k * 128:(k + 1) * 128], ident_b)
                if j == 7:
                    tb.wr = TE(fn, waits=mws)
                else:
                    TE(fn, waits=mws, inc=None)
                mws = []

            def cp(s, tc=tc, half=half):
                dst = xn.rearrange("p (k t) -> p k t", k=KC)[:, half * 8:(half + 1) * 8,
                                                             tc * 128:(tc + 1) * 128]
                src = pt_b.rearrange("p (j t) -> p j t", j=8)
                return s.copy(dst, src)
            bank_read(SC, tb, cp)
        m_tr_prev[buf] = tb.wr
        m_xn_tc.append(last("sc"))
    m_xn_done = last("sc")

    comb_state = {"m_comb": None, "m_combT": None}

    def emit_comb():
        scores3 = small3[:, 0:8]
        msk3 = small3[:, 8:16]
        c2t3 = small3[:, 16:24]
        stat3 = small3[:, 24:28]
        m_comb = None
        xnv = xn.rearrange("p (k t) -> p k t", k=KC)
        for tc in range(16):
            bk = banks[6]
            mms = [(xnv[:, k, tc * 128:(tc + 1) * 128], gw_b[:, k * 8:(k + 1) * 8],
                    pb[6][:, 0:8]) for k in range(KC)]
            mm_group(bk, mms, extra_waits=[m_xn_done])
            bank_read(VE, bk, lambda v: v.reduce_max(stat3[:, 0:1], pb[6][:, 0:8], axis=AX.X),
                      extra_waits=[m_comb])
            VE(lambda v: v.tensor_scalar(stat3[:, 1:2], stat3[:, 0:1], -1.0, None, ALU.mult),
               waits=[last("ve")])
            m_negmax = last("ve")
            m_exp2 = bank_read(SC, bk,
                               lambda s: s.activation(scores3, pb[6][:, 0:8], AF.Exp,
                                                      bias=stat3[:, 1:2]),
                               extra_waits=[m_negmax])
            VE(lambda v: v.reduce_sum(stat3[:, 2:3], scores3, axis=AX.X), waits=[m_exp2])
            VE(lambda v: v.reciprocal(stat3[:, 3:4], stat3[:, 2:3]), waits=[last("ve")])
            VE(lambda v: v.tensor_scalar(scores3, scores3, stat3[:, 3:4], None, ALU.mult),
               waits=[last("ve")])
            VE(lambda v: v.reduce_max(stat3[:, 0:1], scores3, axis=AX.X), waits=[last("ve")])
            VE(lambda v, tc=tc: v.tensor_scalar(comb[:, tc * 8:(tc + 1) * 8], scores3,
                                                stat3[:, 0:1], None, ALU.is_ge), waits=[last("ve")])
            VE(lambda v, tc=tc: v.scalar_tensor_tensor(msk3, comb[:, tc * 8:(tc + 1) * 8],
                                                       -1e30, scores3, ALU.mult, ALU.add),
               waits=[last("ve")])
            VE(lambda v: v.reduce_max(stat3[:, 1:2], msk3, axis=AX.X), waits=[last("ve")])
            VE(lambda v: v.tensor_scalar(c2t3, msk3, stat3[:, 1:2], None, ALU.is_ge),
               waits=[last("ve")])
            VE(lambda v, tc=tc: v.tensor_add(comb[:, tc * 8:(tc + 1) * 8],
                                             comb[:, tc * 8:(tc + 1) * 8], c2t3),
               waits=[last("ve")])
            VE(lambda v, tc=tc: v.tensor_mul(comb[:, tc * 8:(tc + 1) * 8],
                                             comb[:, tc * 8:(tc + 1) * 8], scores3),
               waits=[last("ve")])
            m_comb = last("ve")
        for q in range(4):
            bk = banks[5]
            mws = [m_comb]
            if bk.rd is not None:
                mws.append(bk.rd)
            for j in range(4):
                tc = q * 4 + j

                def fn(t, tc=tc, j=j):
                    return t.transpose(pb[5][0:8, j * 128:(j + 1) * 128],
                                       comb[:, tc * 8:(tc + 1) * 8], ident_f)
                if j == 3:
                    bk.wr = TE(fn, waits=mws)
                else:
                    TE(fn, waits=mws, inc=None)
                mws = []
            bank_read(VE, bk,
                      lambda v, q=q: v.tensor_copy(combT[:, q * 512:(q + 1) * 512], pb[5][0:8, 0:512]))
        comb_state["m_comb"] = m_comb
        comb_state["m_combT"] = last("ve")

    if "comb" in phases and "attn" not in phases:
        emit_comb()
    m_comb = comb_state["m_comb"]
    m_combT = comb_state["m_combT"]

    m_attn_sc = None
    m_kvin = None
    m_attn_final = None
    if "attn" in phases:
        attn = arena("attn", [128, NH * T], BF16, 0)
        wq_buf = [arena(f"wqb{i}", [128, KC * 128], BF16, 65536 + i * 4096) for i in range(2)]
        wk_buf = [arena(f"wkb{i}", [128, KC * 128], BF16, 73728 + i * 4096) for i in range(2)]
        wv_buf = arena("wvb", [128, KC * 256], BF16, 81920)
        qt_t = [arena(f"qtt{i}", [128, G * 512], BF16, 90112 + i * 6144) for i in range(2)]
        kt_t = [arena(f"ktt{i}", [128, G * 512], BF16, 92160 + i * 6144) for i in range(2)]
        v_t = [arena(f"vtt{i}", [128, 4 * 256], BF16, 94208 + i * 6144) for i in range(2)]
        scm_sb = arena("scm_sb", [128, 4 * 512], BF16, 102400)
        ktT_sb = arena("ktT_sb", [128, 4 * 256], BF16, 106496)
        kvtmp = arena("kvtmp", [128, 128], F32, 108544)

        xnv3 = xn.rearrange("p (k t) -> p k t", k=KC)
        m_w_rd = [None, None, None]
        m_tile_rd = [None, None]
        m_qt_dma = [None, None]
        m_kvup = {}
        m_silv_rd = None
        m_kvbf_rd = None
        first_gq_wait = []

        for g in range(NG):
            gw_waits = list(first_gq_wait) + [m_w_rd[0], m_w_rd[1], m_w_rd[2]]
            first_gq_wait = []
            m_wld = []
            for gl in range(G):
                h = g * G + gl
                dq, sq_ = tiled(wq_buf[gl], dWq[h], KC)
                m_wld.append(GQ(lambda gq, dq=dq, sq_=sq_: gq.dma_start(out=dq, in_=sq_),
                                waits=gw_waits))
                gw_waits = []
                dk, sk_ = tiled(wk_buf[gl], dWk[h], KC)
                m_wld.append(GQ(lambda gq, dk=dk, sk_=sk_: gq.dma_start(out=dk, in_=sk_)))
            dv, sv_ = tiled(wv_buf, dWv[g], KC)
            m_wld.append(GQ(lambda gq, dv=dv, sv_=sv_: gq.dma_start(out=dv, in_=sv_)))

            for gl in range(G):
                h = g * G + gl
                for tab, dst in [(qdec_all, qdec_bc), (kdec_all, kdec_bc)]:
                    bw = [m_exp]
                    if banks[5].rd is not None:
                        bw.append(banks[5].rd)
                    TE(lambda t, h=h, tab=tab: t.matmul(
                        pb[5][:, 0:256], sel16_b[:, h * 128:(h + 1) * 128], tab,
                        start=True, stop=True), waits=bw, inc=None)
                    banks[5].wr = TE(lambda t, h=h, tab=tab: t.matmul(
                        pb[5][:, 256:512], sel16_b[:, h * 128:(h + 1) * 128], tab,
                        start=True, stop=True))
                    bank_read(VE, banks[5],
                              lambda v, gl=gl, dst=dst: v.tensor_copy(
                                  dst[:, gl * 512:(gl + 1) * 512], pb[5][:, 0:512]))
            m_dec = last("ve")

            def emit_proj(bp):
                par = bp % 2
                tok = slice(bp * 512, (bp + 1) * 512)
                pw = list(m_wld) + [m_dec, m_tile_rd[par], m_qt_dma[par],
                                    m_xn_tc[bp * 4 + 3]]
                pbi = 0
                m_v = [None, None]

                def v_group(tc2):
                    nonlocal pbi, pw
                    bk = banks[pbi]
                    o = bk.ap[:, 0:256]
                    tsl = slice((bp * 4 + tc2) * 128, (bp * 4 + tc2 + 1) * 128)
                    mms = [(xnv3[:, k, tsl], wv_buf[:, k * 256:(k + 1) * 256], o)
                           for k in range(KC)]
                    mm_group(bk, mms, extra_waits=pw)
                    pw = []
                    m = bank_read(SC, bk, lambda s, tc2=tc2, par=par, o=o: s.activation(
                        v_t[par][:, tc2 * 256:(tc2 + 1) * 256], o, AF.Silu))
                    pbi ^= 1
                    return m

                v_group(0)
                m_v[0] = v_group(1)
                m_qk = [None, None]
                for wbufs, dec, dst in [(wq_buf, qdec_bc, qt_t), (wk_buf, kdec_bc, kt_t)]:
                    for gl in range(G):
                        bk = banks[pbi]
                        o = bk.ap
                        mms = [(wbufs[gl][:, k * 128:(k + 1) * 128], xnv3[:, k, tok], o)
                               for k in range(KC)]
                        mm_group(bk, mms, extra_waits=pw)
                        pw = []
                        b6 = banks[6]
                        sw = [b6.rd] if b6.rd is not None else []
                        m_sil = bank_read(SC, bk, lambda s, o=o: s.activation(
                            b6.ap, o, AF.Silu), extra_waits=sw)
                        b6.wr = m_sil
                        m_qk[gl] = bank_read(VE, b6,
                                             lambda v, gl=gl, par=par, dec=dec, dst=dst:
                                             v.tensor_tensor(
                                                 dst[par][:, gl * 512:(gl + 1) * 512], b6.ap,
                                                 dec[:, gl * 512:(gl + 1) * 512], ALU.mult))
                        pbi ^= 1
                v_group(2)
                m_v[1] = v_group(3)
                m_proj = (m_v, last("ve"), m_qk)
                m_flq = VE(lambda v: v.engine_nop(), waits=[m_proj[1]])
                m_qt_dma[par] = GQ(lambda gq, g=g, bp=bp, par=par: gq.dma_start(
                    out=qt_hbm[g, bp], in_=qt_t[par]), waits=[m_flq])
                return m_proj

            def emit_sc_tr(bp, m_proj):
                par = bp % 2
                tb = banks[7]
                first_t = True
                for bi in range(2):
                    for gl in range(G):
                        sidx = bi * 2 + gl
                        bk = banks[2 + sidx]
                        ksl0 = gl * 512 + bi * 256
                        qsl = slice(gl * 512 + bi * 256, gl * 512 + (bi + 1) * 256)
                        mm_group(bk, [(kt_t[par][:, ksl0:ksl0 + 128], qt_t[par][:, qsl],
                                       bk.ap[:, 0:256])], extra_waits=[m_proj[2][gl]])
                        mm_group(bk, [(kt_t[par][:, ksl0 + 128:ksl0 + 256],
                                       qt_t[par][:, qsl], bk.ap[:, 256:512])])
                        bank_read(VE, bk, lambda v, sidx=sidx, bk=bk: v.tensor_tensor(
                            scm_sb[:, sidx * 512:(sidx + 1) * 512], bk.ap, mask_b, ALU.mult),
                            extra_waits=ld0g)
                        tws = [m_proj[2][gl]]
                        if first_t and tb.rd is not None:
                            tws.append(tb.rd)
                        first_t = False
                        TE(lambda t, ksl0=ksl0, par=par, sidx=sidx: t.transpose(
                            pt_b[:, sidx * 256:sidx * 256 + 128],
                            kt_t[par][:, ksl0:ksl0 + 128], ident_b), waits=tws, inc=None)
                        tb.wr = TE(lambda t, ksl0=ksl0, par=par, sidx=sidx: t.transpose(
                            pt_b[:, sidx * 256 + 128:sidx * 256 + 256],
                            kt_t[par][:, ksl0 + 128:ksl0 + 256], ident_b))
                bank_read(VE, tb, lambda v: v.tensor_copy(ktT_sb, pt_b))
                for gl in range(G):
                    h = g * G + gl
                    VE(lambda v, h=h, gl=gl: v.tensor_copy(
                        kv_bf[:, gl * 128:(gl + 1) * 128], kv[:, h * 128:(h + 1) * 128]),
                       waits=[m_kvup.get(h), m_kvbf_rd])
                return last("ve")

            def emit_rest(b, m_proj, m_sctr):
                nonlocal m_kvbf_rd
                par = (b // 2) % 2
                bi = b % 2
                m_scan_end = None
                for gl in range(G):
                    h = g * G + gl
                    sidx = bi * 2 + gl
                    qsl = slice(gl * 512 + bi * 256, gl * 512 + (bi + 1) * 256)
                    vsl = [slice((bi * 2 + j) * 256 + gl * 128,
                                 (bi * 2 + j) * 256 + gl * 128 + 128) for j in range(2)]
                    kvb = kv_bf[:, sidx * 128:(sidx + 1) * 128]
                    if bi == 1:
                        VE(lambda v, h=h, kvb=kvb: v.tensor_copy(
                            kvb, kv[:, h * 128:(h + 1) * 128]),
                           waits=[m_kvup.get(h), m_kvbf_rd])
                    m_cast = m_sctr if bi == 0 else last("ve")
                    bk = banks[2 + gl]
                    mms = []
                    if b > 0:
                        mms.append((kvb, qt_t[par][:, qsl], bk.ap[:, 0:256]))
                    mms.append((v_t[par][:, vsl[0]],
                                scm_sb[:, sidx * 512:sidx * 512 + 256], bk.ap[:, 0:256]))
                    mms.append((v_t[par][:, vsl[1]],
                                scm_sb[:, sidx * 512 + 256:sidx * 512 + 512], bk.ap[:, 0:256]))
                    mm_group(bk, mms, extra_waits=[m_sctr, m_cast, m_proj[0][bi]])
                    m_kvbf_rd = bk.wr
                    bank_read(SC, bk, lambda s, h=h, b=b, bk=bk: s.copy(
                        attn[:, h * T + b * 256:h * T + (b + 1) * 256], bk.ap[:, 0:256]))
                    ub = banks[4 + gl]
                    mm_group(ub, [
                        (ktT_sb[:, sidx * 256:sidx * 256 + 128], v_t[par][:, vsl[0]],
                         ub.ap[:, 0:128]),
                        (ktT_sb[:, sidx * 256 + 128:sidx * 256 + 256], v_t[par][:, vsl[1]],
                         ub.ap[:, 0:128])], extra_waits=[m_sctr])
                    m_add = bank_read(VE, ub, lambda v, h=h, ub=ub: v.tensor_tensor(
                        kvtmp, kv[:, h * 128:(h + 1) * 128], ub.ap[:, 0:128], ALU.add),
                        extra_waits=[m_kvup.get(h), m_cast])
                    VE(lambda v, h=h: v.tensor_scalar(kv[:, h * 128:(h + 1) * 128], kvtmp,
                                                      bd_bc[:, h:h + 1], None, ALU.mult),
                       waits=[m_add])
                    m_kvup[h] = last("ve")
                    m_scan_end = m_kvup[h]
                return m_scan_end

            mprojs = {0: emit_proj(0)}
            for bp in range(NB // 2):
                m_sctr = emit_sc_tr(bp, mprojs[bp])
                if bp + 1 < NB // 2:
                    mprojs[bp + 1] = emit_proj(bp + 1)
                emit_rest(2 * bp, mprojs[bp], m_sctr)
                m_tile_rd[bp % 2] = emit_rest(2 * bp + 1, mprojs[bp], m_sctr)
            m_w_rd = [banks[4].wr, banks[4].wr, banks[4].wr]
        m_attn_sc = last("sc")

        m_flkv = VE(lambda v: v.engine_nop(), waits=[m_kvup[NH - 1]])
        m_kvB = GQ(lambda gq: gq.dma_start(out=kvB.ap(), in_=kv), waits=[m_flkv])
        plan.on("gpsimd", lambda gq: gq.collective_compute(
            "AllGather", ALU.bypass, replica_groups=[list(range(NCORES))],
            ins=[kvB.ap().opt()], outs=[kvBg.ap().opt()]), waits=[m_kvB], inc="cc", inc_k=1)
        m_cc = ("cc", 1)
        VE(lambda v: v.memset(kv, 0.0), waits=[m_kvB])
        m_kv0b = last("ve")
        if "comb" in phases:
            emit_comb()
        tmpB = arena("tmpB", [128, NH * 128], F32, 90112)
        m_tmpB_rd = None
        for cp in range(7):
            m_t = GQ(lambda gq, cp=cp: gq.dma_start(
                out=tmpB, in_=kvBg.ap()[cp * 128:(cp + 1) * 128, :]),
                waits=[m_cc, m_tmpB_rd])
            for h in range(NH):
                VE(lambda v, cp=cp, h=h: v.tensor_scalar(
                    tmpB[:, h * 128:(h + 1) * 128], tmpB[:, h * 128:(h + 1) * 128],
                    lam_b[:, cp * 16 + h:cp * 16 + h + 1], None, ALU.mult),
                   waits=[m_t] if h == 0 else [])
            m_ts = last("ve")
            VE(lambda v: v.tensor_tensor(kv, kv, tmpB, ALU.add), waits=[m_ts, m_kv0b])
            m_tmpB_rd = last("ve")
        m_kvin = last("ve")

        qtt = [arena(f"qc{i}", [128, G * 512], BF16, 65536 + i * 2048) for i in range(2)]
        kvs_f = [arena(f"kvsf{i}", [128, 128], F32, 69632 + i * 512) for i in range(2)]
        kvs_b = [arena(f"kvsb{i}", [128, 128], BF16, 70656 + i * 256) for i in range(2)]
        m_qtt_rd = [None, None]
        m_corr_first = [m_attn_sc]
        for g in range(NG):
            for gl in range(G):
                h = g * G + gl
                VE(lambda v, gl=gl, h=h: v.tensor_copy(kvs_f[gl], kv[:, h * 128:(h + 1) * 128]),
                   waits=[m_kvin])
                VE(lambda v, gl=gl: v.tensor_copy(kvs_b[gl], kvs_f[gl]), waits=[last("ve")])
            m_kvs = last("ve")
            for bp in range(NB // 2):
                par = bp % 2
                m_q = GQ(lambda gq, g=g, bp=bp, par=par: gq.dma_start(
                    out=qtt[par], in_=qt_hbm[g, bp]),
                    waits=[m_qtt_rd[par]] + ([m_kvin] if (g == 0 and bp == 0) else []))
                for bi in range(2):
                    b = 2 * bp + bi
                    for gl in range(G):
                        h = g * G + gl
                        bk = banks[2 + gl]
                        o = bk.ap[:, 0:256]
                        qsl = slice(gl * 512 + bi * 256, gl * 512 + (bi + 1) * 256)
                        mm_group(bk, [(kvs_b[gl], qtt[par][:, qsl], o)],
                                 extra_waits=[m_q, m_kvs])
                        m_corr = bank_read(VE, bk, lambda v, h=h, b=b, o=o: v.tensor_tensor(
                            attn[:, h * T + b * 256:h * T + (b + 1) * 256],
                            attn[:, h * T + b * 256:h * T + (b + 1) * 256], o, ALU.add),
                            extra_waits=m_corr_first)
                        m_corr_first = []
                        m_qtt_rd[par] = m_corr
                        if b < NB - 1:
                            VE(lambda v, gl=gl, h=h: v.tensor_scalar(
                                kvs_f[gl], kvs_f[gl], bd_bc[:, h:h + 1], None, ALU.mult),
                               waits=[bk.wr])
                            VE(lambda v, gl=gl: v.tensor_copy(kvs_b[gl], kvs_f[gl]),
                               waits=[last("ve")])
                            m_kvs = last("ve")
        m_attn_final = last("ve")
        m_comb = comb_state["m_comb"]
        m_combT = comb_state["m_combT"]

    m_z = None
    if "ffn" in phases:
        attnv = attn.rearrange("p (h t) -> p h t", h=NH)
        sq = [arena(f"sq{i}", [128, D], BF16, 65536 + i * 4096) for i in range(2)]
        rstd_bc = arena("rstd_bc", [128, T], F32, 73728)
        wgate_buf = [arena(f"wgb{i}", [128, KC * 128], BF16, 90112 + i * 4096) for i in range(2)]
        sig = arena("sig", [128, 512], F32, 98304)
        t2 = arena("t2", [128, 512], F32, 100352)

        m_sq_rd = [None, None]
        for k in range(KC):
            par = k % 2
            SC(lambda s, k=k, par=par: s.activation(sq[par], attnv[:, k, :], AF.Square),
               waits=[m_attn_final, m_sq_rd[par]])
            m_sq = last("sc")
            for slab in range(4):
                bk = banks[slab]
                ws = [m_sq]
                if k == 0 and bk.rd is not None:
                    ws.append(bk.rd)
                def fn(t, par=par, slab=slab, k=k):
                    return t.matmul(pb[slab], ones_b, sq[par][:, slab * 512:(slab + 1) * 512],
                                    start=(k == 0), stop=(k == KC - 1))
                if slab == 3:
                    m_sq_rd[par] = TE(fn, waits=ws)
                else:
                    TE(fn, waits=ws, inc=None)
                if k == KC - 1:
                    bk.wr = m_sq_rd[par] if slab == 3 else None
        for slab in range(4):
            banks[slab].wr = m_sq_rd[(KC - 1) % 2]
        for slab in range(4):
            bk = banks[slab]
            m_lnr = bank_read(SC, bk, lambda s, slab=slab: s.activation(
                rstd_bc[:, slab * 512:(slab + 1) * 512], pb[slab], AF.Ln, scale=1.0 / D,
                bias=small1[:, 2:3]))
            SC(lambda s, slab=slab: s.activation(rstd_bc[:, slab * 512:(slab + 1) * 512],
                                                 rstd_bc[:, slab * 512:(slab + 1) * 512],
                                                 AF.Exp, scale=-0.5), waits=[m_lnr])
        m_rstd_bc = last("sc")

        m_wg_rd = [None, None]
        for o in range(KC):
            par = o % 2
            dd, ss = tiled(wgate_buf[par], dWgateT[o], KC)
            m_wl = GQ(lambda gq, dd=dd, ss=ss: gq.dma_start(out=dd, in_=ss),
                      waits=[m_wg_rd[par]])
            for slab in range(4):
                bk = banks[slab % 2]
                o_ap = bk.ap
                mms = [(wgate_buf[par][:, k * 128:(k + 1) * 128],
                        xnv3[:, k, slab * 512:(slab + 1) * 512], o_ap) for k in range(KC)]
                mm_group(bk, mms, extra_waits=[m_wl, m_rstd_bc])
                if slab == 3:
                    m_wg_rd[par] = bk.wr
                m_sg = bank_read(SC, bk, lambda s, o_ap=o_ap: s.activation(sig, o_ap, AF.Sigmoid),
                                 extra_waits=[("ve", plan.ctr.get("ve", 0))])
                VE(lambda v, o=o, slab=slab: v.scalar_tensor_tensor(
                    t2, sig, normw[:, o:o + 1], attnv[:, o, slab * 512:(slab + 1) * 512],
                    ALU.mult, ALU.mult), waits=[m_sg])
                VE(lambda v, o=o, slab=slab: v.tensor_tensor(
                    attnv[:, o, slab * 512:(slab + 1) * 512], t2,
                    rstd_bc[:, slab * 512:(slab + 1) * 512], ALU.mult), waits=[last("ve")])
        m_z = last("ve")

        m_pe1 = last("pe")
        wout_buf = [arena(f"wob{i}", [128, KC * 512], BF16, 65536 + i * 16384) for i in range(2)]
        seed_p = [arena(f"seedp{i}", [128, 1024], F32, 98304 + i * 4096) for i in range(2)]
        m_wo_rd = [None, None]
        m_a1_st = [None, None]
        for d in range(4):
            par = d % 2
            dd = wout_buf[par].rearrange("p (k n) -> p k n", k=KC)
            ss = dWoutT[:, d * 512:(d + 1) * 512].rearrange("(k p) n -> p k n", p=128)
            m_wl = GQ(lambda gq, dd=dd, ss=ss: gq.dma_start(out=dd, in_=ss),
                      waits=[m_wo_rd[par]] + ([m_z, m_pe1] if d < 2 else []))
            for t in range(16):
                bk = banks[t % 2]
                o_ap = bk.ap
                mms = [(attnv[:, o, t * 128:(t + 1) * 128],
                        wout_buf[par][:, o * 512:(o + 1) * 512], o_ap) for o in range(KC)]
                mm_group(bk, mms, extra_waits=[m_wl])
                if t >= 14:
                    m_wo_rd[par] = bk.wr
                tpar = (t // 2) % 2
                half = t % 2
                m_cp = bank_read(SC, bk, lambda s, tpar=tpar, half=half, o_ap=o_ap: s.copy(
                    seed_p[tpar][:, half * 512:(half + 1) * 512], o_ap),
                    extra_waits=[m_a1_st[tpar]] if half == 0 else [])
                if half == 1:
                    m_fl = SC(lambda s: s.copy(small1[0:1, 3:4], small1[0:1, 2:3]),
                              waits=[m_cp])
                    m_a1_st[tpar] = XQ(lambda sy, t=t, d=d, tpar=tpar: sy.dma_start(
                        out=acc_a[(t - 1) * 128:(t + 1) * 128,
                                  d * 512:(d + 1) * 512].rearrange("(g p) n -> p g n", p=128),
                        in_=seed_p[tpar].rearrange("p (g n) -> p g n", g=2)), waits=[m_fl])

        m_dump1 = None
        if dbg:
            m_dump1 = GQ(lambda gq: gq.dma_start(out=dbg_outs["acc1_dbg"], in_=acc_a),
                         waits=[m_a1_st[0], m_a1_st[1]])

        m_pe2 = last("pe")
        gsh = arena("gsh", [128, HC * T], BF16, 0)
        swg_buf = [arena(f"sgb{i}", [128, KC * 128], BF16, 45056 + i * 4096) for i in range(2)]
        swu_buf = [arena(f"sub{i}", [128, KC * 128], BF16, 53248 + i * 4096) for i in range(2)]
        swd_buf = [arena(f"sdb{i}", [128, HC * 512], BF16, 61440 + i * 11264) for i in range(2)]
        upc = arena("upc", [128, 512], BF16, 83968)
        sg_t = arena("sg_t", [128, 512], BF16, 84992)
        accin_p = [arena(f"acip{i}", [128, 1024], F32, 86016 + i * 4096) for i in range(2)]
        accout_p = [arena(f"acop{i}", [128, 1024], F32, 94208 + i * 4096) for i in range(2)]
        combc = arena("combc", [128, T], BF16, 102400)
        xin_p = arena("xin_p", [128, 1024], F32, 106496)
        m_acc_st = [None, m_a1_st[0]]
        m_aci_rd2 = [None, None]
        m_xin_rd = [None]
        m_sg_rd = [None, None]
        m_su_rd = [None, None]
        m_sd_rd = [None, None]
        m_aci_rd = [None, None]
        m_aco_st = [None, None]
        m_sgt_rd = None
        m_upc_rd = None

        def gup_preload(dG, dU, hp, fw=()):
            par = hp % 2
            dd, ss = tiled(swg_buf[par], dG[hp], KC)
            m = [GQ(lambda gq, dd=dd, ss=ss: gq.dma_start(out=dd, in_=ss),
                    waits=[m_sg_rd[par]] + list(fw))]
            dd, ss = tiled(swu_buf[par], dU[hp], KC)
            m.append(GQ(lambda gq, dd=dd, ss=ss: gq.dma_start(out=dd, in_=ss),
                        waits=[m_su_rd[par]]))
            return m

        def gup_phase(nchunks, dG, dU, gdst, e=None, with_comb=False, first_waits=(),
                      preloaded=None):
            nonlocal m_sgt_rd, m_upc_rd
            fw = list(first_waits)
            preloaded = dict(preloaded or {})
            for hp in range(nchunks):
                par = hp % 2
                if hp in preloaded:
                    m_wl = preloaded[hp] + fw
                    fw = []
                else:
                    m_wl = gup_preload(dG, dU, hp, fw)
                    fw = []
                for slab in range(4):
                    bg = banks[slab % 2]
                    bu = banks[2 + slab % 2]
                    og = bg.ap
                    ou = bu.ap
                    mg = [(swg_buf[par][:, k * 128:(k + 1) * 128],
                           xnv3[:, k, slab * 512:(slab + 1) * 512], og) for k in range(KC)]
                    mu = [(swu_buf[par][:, k * 128:(k + 1) * 128],
                           xnv3[:, k, slab * 512:(slab + 1) * 512], ou) for k in range(KC)]
                    mm_group(bg, mg, extra_waits=m_wl)
                    if slab == 3:
                        m_sg_rd[par] = bg.wr
                    mm_group(bu, mu)
                    if slab == 3:
                        m_su_rd[par] = bu.wr
                    m_sil = bank_read(SC, bg, lambda s, og=og: s.activation(sg_t, og, AF.Silu),
                                      extra_waits=[m_sgt_rd])
                    if with_comb:
                        m_u = bank_read(VE, bu, lambda v, ou=ou, slab=slab: v.tensor_tensor(
                            upc, ou, combc[:, slab * 512:(slab + 1) * 512], ALU.mult),
                            extra_waits=[m_upc_rd])
                        VE(lambda v, hp=hp, slab=slab, gdst=gdst: v.tensor_tensor(
                            gdst[:, hp * T + slab * 512:hp * T + (slab + 1) * 512],
                            sg_t, upc, ALU.mult), waits=[m_sil, m_u])
                        m_upc_rd = last("ve")
                        m_sgt_rd = m_upc_rd
                    else:
                        m_gv = bank_read(VE, bu, lambda v, ou=ou, hp=hp, slab=slab, gdst=gdst:
                                         v.tensor_tensor(
                                             gdst[:, hp * T + slab * 512:hp * T + (slab + 1) * 512],
                                             sg_t, ou, ALU.mult), extra_waits=[m_sil])
                        m_sgt_rd = m_gv
            return last("ve")

        def down_phase(nchunks, dD_slice_fn, gsrc, mode, m_gsrc=None, first_waits=(),
                       hook=None):
            gate = [m_gsrc]
            fw = list(first_waits)

            def swd_load(d):
                par = d % 2
                s2 = dD_slice_fn(d)
                dd = swd_buf[par].rearrange("p (c n) -> p c n", c=nchunks)[:, 0:nchunks, :]
                ss = s2.rearrange("(c p) n -> p c n", p=128)
                return GQ(lambda gq, dd=dd, ss=ss: gq.dma_start(out=dd, in_=ss),
                          waits=[m_sd_rd[par]])

            wl = {0: swd_load(0)}
            for d in range(4):
                par = d % 2
                m_wl = wl.pop(d)
                for t in range(16):
                    bk = banks[4 + t % 3]
                    o_ap = bk.ap
                    tpar = (t // 2) % 2
                    half = t % 2
                    if half == 0:
                        GQ(lambda gq, t=t, d=d, tpar=tpar: gq.dma_start(
                            out=accin_p[tpar].rearrange("p (g n) -> p g n", g=2),
                            in_=acc_a[t * 128:(t + 2) * 128,
                                      d * 512:(d + 1) * 512].rearrange("(g p) n -> p g n", p=128)),
                           waits=[m_aci_rd2[tpar]] + fw)
                        fw = []
                        m_aci_ld = last(GQ_POOL[(_qi["g"] - 1) % len(GQ_POOL)])
                        if mode == "final":
                            XQ(lambda sy, t=t, d=d: sy.dma_start(
                                out=xin_p.rearrange("p (g n) -> p g n", g=2),
                                in_=dx[t * 128:(t + 2) * 128,
                                       d * 512:(d + 1) * 512].rearrange("(g p) n -> p g n", p=128)),
                               waits=[m_xin_rd[0]])
                            m_xin_ld = last(XQ_POOL[(_qi["x"] - 1) % len(XQ_POOL)])
                    mms = [(gsrc[:, hp * T + t * 128:hp * T + (t + 1) * 128],
                            swd_buf[par][:, hp * 512:(hp + 1) * 512], o_ap)
                           for hp in range(nchunks)]
                    mm_group(bk, mms, extra_waits=[m_wl] + gate)
                    gate = []
                    if t == 7 and d + 1 < 4:
                        wl[d + 1] = swd_load(d + 1)
                    if t == 15:
                        m_sd_rd[par] = bk.wr
                        if d == 2 and hook is not None:
                            hook()
                    hsl = slice(half * 512, (half + 1) * 512)
                    m_tt = bank_read(VE, bk, lambda v, tpar=tpar, hsl=hsl, o_ap=o_ap:
                                     v.tensor_tensor(accout_p[tpar][:, hsl], o_ap,
                                                     accin_p[tpar][:, hsl], ALU.add),
                                     extra_waits=[m_aci_ld] + ([m_acc_st[tpar]] if half == 0 else []))
                    if mode == "final":
                        m_tt = VE(lambda v, tpar=tpar, hsl=hsl: v.tensor_tensor(
                            accout_p[tpar][:, hsl], accout_p[tpar][:, hsl], xin_p[:, hsl],
                            ALU.add), waits=[m_tt, m_xin_ld])
                    if half == 1:
                        m_aci_rd2[tpar] = m_tt
                        if mode == "final":
                            m_xin_rd[0] = m_tt
                        m_fl = VE(lambda v: v.engine_nop(), waits=[m_tt])
                        dst = dout if mode == "final" else acc_a
                        XQ(lambda sy, t=t, d=d, tpar=tpar, dst=dst: sy.dma_start(
                            out=dst[(t - 1) * 128:(t + 1) * 128,
                                    d * 512:(d + 1) * 512].rearrange("(g p) n -> p g n", p=128),
                            in_=accout_p[tpar].rearrange("p (g n) -> p g n", g=2)),
                           waits=[m_fl])
                        m_acc_st[tpar] = last(XQ_POOL[(_qi["x"] - 1) % len(XQ_POOL)])

        m_gsh = gup_phase(HC, [dSwg[hp] for hp in range(HC)], [dSwu[hp] for hp in range(HC)],
                          gsh, first_waits=[m_pe2])
        if dbg:
            m_gsh = GQ(lambda gq: gq.dma_start(out=dbg_outs["gsh_dbg"], in_=gsh),
                       waits=[m_gsh])
        down_phase(HC, lambda d: dSwd[0:1408, d * 512:(d + 1) * 512], gsh, "accum",
                   m_gsrc=m_gsh, first_waits=[m_a1_st[0], m_a1_st[1], m_dump1])
        m_gsh = gup_phase(HC, [dSwg[HC + hp] for hp in range(HC)],
                          [dSwu[HC + hp] for hp in range(HC)], gsh)
        down_phase(HC, lambda d: dSwd[1408:2816, d * 512:(d + 1) * 512], gsh, "accum",
                   m_gsrc=m_gsh, first_waits=[m_acc_st[0], m_acc_st[1]])

        m_dump2 = None
        if dbg:
            m_dump2 = GQ(lambda gq: gq.dma_start(out=dbg_outs["acc2_dbg"], in_=acc_a),
                         waits=[m_acc_st[0], m_acc_st[1]])

        pre_g = {}
        for e in range(E):
            for slab in range(4):
                bk = banks[6]
                mm_group(bk, [(sel8_b[:, e * 128:(e + 1) * 128],
                               combT[:, slab * 512:(slab + 1) * 512], pb[6])],
                         extra_waits=[m_combT])
                bank_read(VE, bk, lambda v, slab=slab: v.tensor_copy(
                    combc[:, slab * 512:(slab + 1) * 512], pb[6]),
                    extra_waits=[m_upc_rd] + ([m_a1_st[1]] if e == 0 and slab == 0 else []))
            m_cbc = last("ve")
            m_upc_rd = m_cbc
            m_gc = gup_phase(HC, [dWg[e, hp] for hp in range(HC)],
                             [dWu[e, hp] for hp in range(HC)],
                             gsh, with_comb=True, first_waits=[m_cbc],
                             preloaded=pre_g.pop(e, None))
            is_last = (e == E - 1)
            if dbg and e == 0:
                m_gc = GQ(lambda gq: gq.dma_start(out=dbg_outs["gc_dbg"], in_=gsh),
                          waits=[m_gc])
            def prefetch_next(e=e):
                if e + 1 < E:
                    pre_g[e + 1] = {hp: gup_preload([dWg[e + 1, h2] for h2 in range(HC)],
                                                    [dWu[e + 1, h2] for h2 in range(HC)], hp)
                                    for hp in range(2)}
            down_phase(HC, lambda d, e=e: dWd[e][:, d * 512:(d + 1) * 512], gsh,
                       "final" if is_last else "accum", m_gsrc=m_gc,
                       first_waits=[m_acc_st[0], m_acc_st[1]] + ([m_dump2] if e == 0 else []),
                       hook=prefetch_next)

    if dbg:
        GQ(lambda g: g.dma_start(out=dbg_outs["xn_fm"], in_=xn), waits=[m_xn_done])
        if m_attn_final is not None and "ffn" not in phases:
            GQ(lambda g: g.dma_start(out=dbg_outs["attn_fm"], in_=attn), waits=[m_attn_final])
        if m_attn_final is not None:
            GQ(lambda g: g.dma_start(out=dbg_outs["kvin_dbg"], in_=kv), waits=[m_kvin])
        if m_z is not None:
            GQ(lambda g: g.dma_start(out=dbg_outs["z_fm"], in_=attn), waits=[m_z])
            GQ(lambda g: g.dma_start(out=dbg_outs["rstd_dbg"], in_=rstd_bc),
               waits=[last("sc")])
        if m_comb is not None:
            GQ(lambda g: g.dma_start(out=dbg_outs["comb_dbg"], in_=comb), waits=[m_comb])
            GQ(lambda g: g.dma_start(out=dbg_outs["small2_dbg"], in_=small2),
               waits=[m_comb, last("sc")])
        if "ffn" not in phases:
            XQ(lambda sy: sy.dma_start(out=dout, in_=dx))
    plan.on("sync", lambda sy: sy.nop(), waits=all_dma_done(), inc=None)

    import contextlib
    with contextlib.ExitStack() as es:
        sems = {s: es.enter_context(nc.semaphore(s)) for s in SEMS}
        with nc.Block() as block:
            def mk(ename):
                def body(eng):
                    for waits, fn, inc, k in plan.items[ename]:
                        for (s, v) in waits:
                            eng.wait_ge(sems[s], v)
                        instr = fn(eng)
                        if inc is not None:
                            instr.then_inc(sems[inc], k)
                return body
            block.sync(mk("sync"))
            block.tensor(mk("tensor"))
            block.vector(mk("vector"))
            block.scalar(mk("scalar"))
            block.gpsimd(mk("gpsimd"))
    return nc



def _in_maps(L0_attn_hidden_states, L0_attn_slope_rate, L0_attn_w_qkv, L0_attn_w_gate,
             L0_attn_w_out, L0_attn_norm_w, L0_ffn_gate_w, L0_ffn_w_gate, L0_ffn_w_up,
             L0_ffn_w_down, L0_ffn_sw_gate, L0_ffn_sw_up, L0_ffn_sw_down):
    shared = _prep_shared(L0_attn_w_qkv, L0_attn_w_gate, L0_attn_w_out, L0_ffn_gate_w,
                          L0_ffn_w_gate, L0_ffn_w_up, L0_ffn_w_down, L0_ffn_sw_gate,
                          L0_ffn_sw_up, L0_ffn_sw_down, L0_attn_norm_w)
    ins = []
    for c in range(NCORES):
        m = dict(shared)
        m.update(_prep_core(c, L0_attn_hidden_states, L0_attn_slope_rate))
        ins.append(m)
    return ins


def kernel(**inputs):
    in_maps = _in_maps(**inputs)
    nc = build_program(dbg=DBG)
    res = run_bass_kernel_spmd(nc, in_maps, core_ids=list(range(NCORES)))
    out = np.concatenate([r["out"] for r in res.results], axis=0).reshape(1, NCORES * T, D)
    if DBG:
        kernel.dbg_results = res.results
    return out.astype(np.float32)
